# version 15
# baseline (speedup 1.0000x reference)
import math
from contextlib import ExitStack

import numpy as np
import ml_dtypes

import concourse.bass as bass
import concourse.mybir as mybir
from concourse.bass_utils import run_bass_kernel_spmd

F32 = mybir.dt.float32
BF16 = mybir.dt.bfloat16
AF = mybir.ActivationFunctionType
ALU = mybir.AluOpType
AX = mybir.AxisListType
BF = ml_dtypes.bfloat16

DM = 4096
SEQ = 4096
BATCH = 2
NH = 32
NKV = 8
HD = 128
IFD = 11008
NCORE = 8
TOK = 1024
NSLOT = 8
NCI = IFD // 128
NDMC = DM // 128
MAGIC = 12582912.0
NEGBIG = -1e30
NWEL = float(IFD) * float(DM)
EPS_RMS = 1e-6
EPS_LN = 1e-5
EPS_BIT = 1e-5
RSQ_HD = 1.0 / math.sqrt(HD)

QT_GU = [640] * 17 + [128]
DEBUG_P2_UNITS = None


def _slopes64():
    start = 2 ** (-(2 ** (-(math.log2(NH) - 3))))
    return np.array([start * start ** i for i in range(NH)], dtype=np.float64)


def slot_bj(t, c):
    return t & 1, 8 * (t >> 1) + c


class Sem:
    __slots__ = ("name", "h", "n")

    def __init__(self, name):
        self.name = name
        self.h = None
        self.n = 0

    def inc(self, k=1):
        self.n += k
        return self.n


class Res:
    __slots__ = ("g",)

    def __init__(self):
        self.g = []

    def read_by(self, sem, n):
        self.g.append((sem, n))


class KB:
    ENGINES = ("sync", "gpsimd", "tensor", "scalar", "vector")

    def __init__(self):
        self.nc = bass.Bass("TRN2", target_bir_lowering=False, num_devices=NCORE)
        self.streams = {e: [] for e in self.ENGINES}
        self.sems = {}

    def sem(self, name):
        s = self.sems.get(name)
        if s is None:
            s = Sem(name)
            self.sems[name] = s
        return s

    def em(self, eng, fn):
        self.streams[eng].append(fn)

    def wait(self, eng, sem, val):
        if val > 0:
            self.em(eng, lambda E, s=sem, v=val: E.wait_ge(s.h, v))

    def guard(self, eng, res):
        for (s, n) in res.g:
            self.wait(eng, s, n)
        res.g = []

    def dma(self, q, out_ap, in_ap, sem):
        n = sem.inc(16)
        self.em(q, lambda E, o=out_ap, i=in_ap, s=sem:
                E.dma_start(o, i).then_inc(s.h, 16))
        return n

    def mm(self, out, lhsT, rhs, start, stop, sig=False):
        sem = self.sem("pe") if (sig or stop) else None
        n = sem.inc() if sem else None

        def f(E, o=out, l=lhsT, r=rhs, st=start, sp=stop, s=sem):
            ins = E.matmul(o, l, r, start=st, stop=sp)
            if s is not None:
                ins.then_inc(s.h, 1)
        self.em("tensor", f)
        return n

    def tr(self, out, in_, ident):
        sem = self.sem("pe")
        n = sem.inc()

        def f(E, o=out, i=in_, d=ident, s=sem):
            E.transpose(o, i, d).then_inc(s.h, 1)
        self.em("tensor", f)
        return n

    def act(self, out, in_, func, bias=0.0, scale=1.0, accum_out=None):
        sem = self.sem("act")
        n = sem.inc()

        def f(E, o=out, i=in_, fu=func, b=bias, sc=scale, ao=accum_out, s=sem):
            E.activation(o, i, fu, bias=b, scale=sc, accum_out=ao).then_inc(s.h, 1)
        self.em("scalar", f)
        return n

    def vop(self, fn):
        self.em("vector", fn)

    def vsig(self):
        s = self.sem("dve")
        n = s.inc()
        self.em("vector", lambda E, ss=s: E.drain().then_inc(ss.h, 1))
        return n

    def quiesce(self, queues=("sync", "gpsimd")):
        for qn in queues:
            for sname in ("pe", "act", "dve", "pool"):
                if sname in self.sems:
                    self.wait(qn, self.sems[sname], self.sems[sname].n)

    def emit(self, block):
        for name in self.ENGINES:
            fns = self.streams[name]

            def runner(E, fns=fns):
                for fn in fns:
                    fn(E)
            getattr(block, name)(runner)


def build(taps=()):
    taps = set(taps)
    k = KB()
    nc = k.nc
    S = k.sem

    XT = nc.dram_tensor("XT", [DM, TOK], F32, kind="ExternalInput").ap()
    WQT = nc.dram_tensor("WQT", [NH, DM, HD], F32, kind="ExternalInput").ap()
    WKT = nc.dram_tensor("WKT", [NKV, DM, HD], F32, kind="ExternalInput").ap()
    WVT = nc.dram_tensor("WVT", [2, DM, 512], F32, kind="ExternalInput").ap()
    WOT = nc.dram_tensor("WOT", [NDMC, DM, 128], F32, kind="ExternalInput").ap()
    WG = nc.dram_tensor("WG", [512, IFD], F32, kind="ExternalInput").ap()
    WU = nc.dram_tensor("WU", [512, IFD], F32, kind="ExternalInput").ap()
    WD = nc.dram_tensor("WD", [IFD, 512], F32, kind="ExternalInput").ap()
    QNW = nc.dram_tensor("QNW", [128, 1], F32, kind="ExternalInput").ap()
    KNW = nc.dram_tensor("KNW", [128, 1], F32, kind="ExternalInput").ap()
    MSK1 = nc.dram_tensor("MSK1", [128, 1024], F32, kind="ExternalInput").ap()
    MSK2 = nc.dram_tensor("MSK2", [128, 1024], F32, kind="ExternalInput").ap()
    OUT = nc.dram_tensor("OUT", [DM, TOK], F32, kind="ExternalOutput").ap()

    POSD = nc.inline_tensor(np.arange(SEQ, dtype=np.float32).reshape(1, SEQ),
                            name="posrow_d").ap()
    pos2_np = np.ones((2, SEQ), dtype=np.float32)
    pos2_np[1, :] = np.arange(SEQ, dtype=np.float32)
    POS2D = nc.inline_tensor(pos2_np, name="pos2_d").ap()
    ONERD = nc.inline_tensor(np.ones((1, 128), dtype=np.float32),
                             name="onerow_d").ap()
    slopes64 = _slopes64()
    slopea_np = np.zeros((1, SEQ), dtype=np.float32)
    for h in range(NH):
        slopea_np[0, h * 128:(h + 1) * 128] = slopes64[h]
    SLOPEAD = nc.inline_tensor(slopea_np, name="slopea_d").ap()
    posi = np.arange(SEQ)
    poshl_np = np.zeros((2, SEQ), dtype=BF)
    poshl_np[0, :] = (posi >> 5).astype(np.float32)
    poshl_np[1, :] = (posi & 31).astype(np.float32)
    POSHLD = nc.inline_tensor(poshl_np, name="poshl_d").ap()
    pos4_np = np.ones((4, SEQ), dtype=BF)
    pos4_np[2, :] = poshl_np[0, :]
    pos4_np[3, :] = poshl_np[1, :]
    POS4D = nc.inline_tensor(pos4_np, name="pos4_d").ap()
    slb16 = slopes64.astype(BF)
    sl2_np = np.zeros((2, SEQ), dtype=BF)
    for h in range(NH):
        sl2_np[0, h * 128:(h + 1) * 128] = BF(np.float32(slb16[h]) * 32.0)
        sl2_np[1, h * 128:(h + 1) * 128] = slb16[h]
    SL2D = nc.inline_tensor(sl2_np, name="sl2_d").ap()
    CONSTD = nc.inline_tensor(
        np.repeat(np.array([[EPS_RMS, EPS_LN]], dtype=np.float32), 128, axis=0),
        name="consts_d").ap()
    GE01D = nc.inline_tensor(
        (np.arange(128)[:, None] >= np.arange(128)[None, :]).astype(BF), name="ge01").ap()
    IDFD = nc.inline_tensor(np.eye(128, dtype=np.float32), name="identf").ap()
    IDBD = nc.inline_tensor(np.eye(128, dtype=BF), name="identb").ap()
    ONESBD = nc.inline_tensor(np.ones((128, 1), dtype=BF), name="onesb").ap()
    ONESFD = nc.inline_tensor(np.ones((128, 1), dtype=np.float32), name="onesf").ap()

    QSC = nc.dram_tensor("qsc", [NH * 128, TOK], BF16, kind="Internal").ap()
    K_IN = nc.dram_tensor("k_in", [NKV * 128, TOK], BF16, kind="Internal").ap()
    K_OUT = nc.dram_tensor("k_out", [NCORE * NKV * 128, TOK], BF16, kind="Internal").ap()
    V_IN = nc.dram_tensor("v_in", [NKV * TOK, HD], BF16, kind="Internal").ap()
    V_OUT = nc.dram_tensor("v_out", [NCORE * NKV * TOK, HD], BF16, kind="Internal").ap()
    FGU_IN = nc.dram_tensor("fgu_in", [2 * 4 * IFD, 128], BF16, kind="Internal").ap()
    FGU_OUT = nc.dram_tensor("fgu_out", [NCORE * 2 * 4 * IFD, 128], BF16,
                             kind="Internal").ap()
    FD_IN = nc.dram_tensor("fd_in", [4 * IFD, 128], BF16, kind="Internal").ap()
    FD_OUT = nc.dram_tensor("fd_out", [NCORE * 4 * IFD, 128], BF16,
                            kind="Internal").ap()
    AR_IN = nc.dram_tensor("ar_in", [1, 8], F32, kind="Internal").ap()
    AR_OUT = nc.dram_tensor("ar_out", [1, 8], F32, kind="Internal").ap()
    X1SC = nc.dram_tensor("x1sc", [DM, TOK], F32, kind="Internal").ap()
    X1N = nc.dram_tensor("x1n", [DM, TOK], F32, kind="Internal").ap()
    X2SC = nc.dram_tensor("x2sc", [DM, TOK], F32, kind="Internal").ap()
    INTER = nc.dram_tensor("inter", [NCI * 128, TOK], BF16, kind="Internal").ap()

    RG = [list(range(NCORE))]
    FGU_RANK = 2 * 4 * IFD
    FD_RANK = 4 * IFD

    tap_list = []

    def tap(name, src_ap):
        if name in taps:
            t = nc.dram_tensor("TAP_" + name, list(src_ap.shape), F32,
                               kind="ExternalOutput").ap()
            tap_list.append((t, src_ap))

    with ExitStack() as top:
        sbt = lambda n, s, d: top.enter_context(nc.sbuf_tensor(n, s, d))

        ONEROW = sbt("onerow", [1, 128], F32)
        CONSTS = sbt("consts", [128, 2], F32)
        GE01 = sbt("ge01s", [128, 128], BF16)
        IDF = sbt("idfs", [128, 128], F32)
        IDB = sbt("idbs", [128, 128], BF16)
        ONESB = sbt("onesbs", [128, 1], BF16)
        ONESF = sbt("onesfs", [128, 1], F32)
        MSK1S = sbt("msk1s", [128, 1024], BF16)
        MSK2S = sbt("msk2s", [128, 1024], BF16)
        QNWS = sbt("qnws", [128, 1], F32)
        KNWS = sbt("knws", [128, 1], F32)
        QNWSC = sbt("qnwsc", [128, 1], F32)
        INVC = sbt("invc", [128, 4], F32)
        S3C = sbt("s3c", [128, 1], F32)
        SCROW = sbt("scrow", [1, 8], F32)
        SCSUM = sbt("scsum", [1, 8], F32)
        SCT = sbt("sct", [1, 8], F32)
        ABSM = sbt("absm", [128, 4], F32)
        ABST = sbt("abst", [128, 1], F32)
        SPC = sbt("spc", [128, 1], F32)
        DUMP = sbt("dump", [128, 1024], F32)
        NM = sbt("nm", [128, 16], F32)
        EBF = sbt("ebf", [128, 16], F32)
        NMA = sbt("nma", [128, 4], F32)
        TT4 = sbt("tt4", [128, 4], F32)
        SIGB = sbt("sigb", [128, 4], F32)
        LNSB = sbt("lnsb", [128, 4], F32)
        BCOL = sbt("bcol", [128, 12], F32)
        BROW4 = sbt("brow4", [1, 512], F32)

        MBB = sbt("mbb", [128, 1024], F32)
        IBB = sbt("ibb", [128, 1024], F32)
        gmbb = Res()
        ao_stack = ExitStack()
        AO = ao_stack.enter_context(
            nc.sbuf_tensor("ao", [128, NH * TOK], BF16))

        g = {name: Res() for name in (
            "psq0", "psq1", "psst", "psbb", "qraw", "sqb", "srow", "irow",
            "chb0", "chb1", "dump", "qa0", "qa1", "wb0", "wb1", "vw0", "vw1",
            "kst0", "kst1", "vst0", "vst1", "scrow", "absm",
        )}

        n_sm = 0
        nc.const_aps.register(F32, EPS_RMS, CONSTS[:, 0:1])
        nc.const_aps.register(F32, EPS_LN, CONSTS[:, 1:2])
        for dst, src in ((ONEROW[:, :], ONERD), (CONSTS[:, :], CONSTD),
                         (GE01[:, :], GE01D), (IDF[:, :], IDFD),
                         (IDB[:, :], IDBD),
                         (ONESB[:, :], ONESBD), (ONESF[:, :], ONESFD),
                         (QNWS[:, :], QNW), (KNWS[:, :], KNW)):
            n_sm = k.dma("sync", dst, src, S("dsm"))
        n_m1 = k.dma("sync", DUMP[:, :], MSK1, S("dsm"))
        k.wait("vector", S("dsm"), n_m1)
        k.vop(lambda E: E.tensor_scalar(QNWSC[:, :], QNWS[:, :], RSQ_HD, None, ALU.mult))
        k.vop(lambda E: E.tensor_copy(MSK1S[:, :], DUMP[:, :]))
        n_c1 = k.vsig()
        k.wait("sync", S("dve"), n_c1)
        n_sm = k.dma("sync", DUMP[:, :], MSK2, S("dsm"))
        k.wait("vector", S("dsm"), n_sm)
        k.vop(lambda E: E.tensor_copy(MSK2S[:, :], DUMP[:, :]))
        n_mskcast = k.vsig()

        with ExitStack() as p1:
            sb1 = lambda n, s, d: p1.enter_context(nc.sbuf_tensor(n, s, d))
            xb = sb1("xb", [128, NDMC * TOK], BF16)
            WB = [sb1("wb0", [128, NDMC * 128], BF16),
                  sb1("wb1", [128, NDMC * 128], BF16)]
            KST = [sb1("kst0", [128, 512], BF16), sb1("kst1", [128, 512], BF16)]
            VW = [sb1("vw0", [128, 512], BF16), sb1("vw1", [128, 512], BF16)]
            VST = [sb1("vst0", [128, 512], BF16), sb1("vst1", [128, 512], BF16)]
            QRAW = sb1("qraw", [128, 512], F32)
            SQB = sb1("sqb", [128, 512], BF16)
            QA = [sb1("qa0", [128, 640], F32), sb1("qa1", [128, 640], F32)]
            SROW = sb1("srow", [1, 512], F32)
            IROW = sb1("irow", [1, 512], F32)

            n_xb = 0
            for piece in range(4):
                src = XT[piece * 1024:(piece + 1) * 1024, :].rearrange(
                    "(a p) t -> p a t", p=128)
                dst = xb[:, piece * 8 * TOK:(piece + 1) * 8 * TOK].rearrange(
                    "p (a t) -> p a t", t=TOK)
                n_xb = k.dma("gpsimd", dst, src, S("dxb"))

            qa_tiles = []
            for m, wsrc in ((0, WG), (1, WU)):
                for rc in range(4):
                    c0 = 0
                    for w in QT_GU:
                        qa_tiles.append((m, wsrc, rc, c0, w))
                        c0 += w
            for rc in range(NCI):
                qa_tiles.append((2, WD, rc, 0, 512))
            qa_nload = {}

            def qa_load(i):
                m, wsrc, rc, c0, w = qa_tiles[i]
                buf = QA[i % 2]
                k.guard("gpsimd", g["qa%d" % (i % 2)])
                qa_nload[i] = k.dma("gpsimd", buf[:, 0:w],
                                    wsrc[rc * 128:(rc + 1) * 128, c0:c0 + w],
                                    S("dqa%d" % (i % 2)))

            def qa_reduce(i):
                m, wsrc, rc, c0, w = qa_tiles[i]
                buf = QA[i % 2]
                k.wait("vector", S("dqa%d" % (i % 2)), qa_nload[i])
                first = (i == 0) or (m != qa_tiles[i - 1][0])
                if first:
                    k.vop(lambda E, b=buf, ww=w, mm_=m: E.reduce_sum(
                        ABSM[:, mm_:mm_ + 1], b[:, 0:ww], AX.X,
                        apply_absolute_value=True))
                else:
                    k.vop(lambda E, b=buf, ww=w: E.reduce_sum(
                        ABST[:, :], b[:, 0:ww], AX.X, apply_absolute_value=True))
                    k.vop(lambda E: E.tensor_copy(SPC[:, :], SPC[:, :]))
                    k.vop(lambda E, mm_=m: E.tensor_tensor(
                        ABSM[:, mm_:mm_ + 1], ABSM[:, mm_:mm_ + 1], ABST[:, :],
                        ALU.add))
                nv = k.vsig()
                g["qa%d" % (i % 2)].read_by(S("dve"), nv)

            qa_load(0)
            qa_load(1)
            qa_state = [2, 0]

            def qa_step():
                if qa_state[1] < len(qa_tiles):
                    qa_reduce(qa_state[1])
                    qa_state[1] += 1
                if qa_state[0] < len(qa_tiles):
                    qa_load(qa_state[0])
                    qa_state[0] += 1

            def wblock_load(bi, src3, fc, sem):
                k.guard("gpsimd", g["wb%d" % bi])
                w = src3.shape[2]
                src = src3[fc].rearrange("(a p) f -> p a f", p=128)
                dst = WB[bi][:, 0:NDMC * w].rearrange("p (a f) -> p a f", f=w)
                return k.dma("gpsimd", dst, src, sem)

            with ExitStack() as pp:
                ps = lambda n, s: pp.enter_context(nc.psum_tensor(n, s, F32))
                psQ = [ps("psq0", [128, 512]), ps("psq1", [128, 512])]
                psSt = ps("psst", [1, 512])
                psB = ps("psbb", [128, 512])

                def proj_norm(idx, WBu, gwb, wsem, nw, nrm_w, dst_dram,
                              dst_row, dsem):
                    for half in range(2):
                        par = (idx * 2 + half) % 2
                        pq = psQ[par]
                        gq = g["psq%d" % par]
                        k.wait("tensor", wsem, nw)
                        k.guard("tensor", gq)
                        for dmc in range(NDMC):
                            k.mm(pq[:, :], WBu[:, dmc * 128:(dmc + 1) * 128],
                                 xb[:, dmc * TOK + half * 512: dmc * TOK + half * 512 + 512],
                                 start=(dmc == 0), stop=(dmc == NDMC - 1))
                        npe = k.sems["pe"].n
                        gwb.read_by(S("pe"), npe)
                        k.wait("scalar", S("pe"), npe)
                        k.guard("scalar", g["sqb"])
                        k.act(SQB[:, :], pq[:, :], AF.Square)
                        k.guard("scalar", g["qraw"])
                        na1 = k.act(QRAW[:, :], pq[:, :], AF.Identity)
                        gq.read_by(S("act"), na1)
                        k.wait("tensor", S("act"), na1)
                        k.guard("tensor", g["psst"])
                        k.mm(psSt[:, :], ONESB[:, :], SQB[:, :], start=True, stop=True)
                        npe2 = k.sems["pe"].n
                        g["sqb"].read_by(S("pe"), npe2)
                        k.wait("scalar", S("pe"), npe2)
                        k.guard("scalar", g["srow"])
                        na2 = k.act(SROW[:, :], psSt[:, :], AF.Sqrt,
                                    scale=1.0 / HD, bias=EPS_RMS)
                        g["psst"].read_by(S("act"), na2)
                        k.wait("vector", S("act"), na2)
                        k.guard("vector", g["irow"])
                        k.vop(lambda E: E.reciprocal(IROW[:, :], SROW[:, :]))
                        nv = k.vsig()
                        g["srow"].read_by(S("dve"), nv)
                        k.wait("tensor", S("dve"), nv)
                        k.guard("tensor", g["psbb"])
                        k.mm(psB[:, :], ONEROW[:, :], IROW[:, :],
                             start=True, stop=True)
                        npe3 = k.sems["pe"].n
                        g["irow"].read_by(S("pe"), npe3)
                        k.wait("vector", S("pe"), npe3)
                        k.vop(lambda E: E.tensor_tensor(QRAW[:, :], QRAW[:, :],
                                                        psB[:, :], ALU.mult))
                        ksti = (idx * 2 + half) % 2
                        k.guard("vector", g["kst%d" % ksti])
                        k.vop(lambda E, d=KST[ksti]: E.tensor_scalar(
                            d[:, :], QRAW[:, :], nrm_w[:, :], None, ALU.mult))
                        nv2 = k.vsig()
                        k.wait("sync", S("dve"), nv2)
                        nst = k.dma("sync",
                                    dst_dram[dst_row * 128:(dst_row + 1) * 128,
                                             half * 512:half * 512 + 512],
                                    KST[ksti][:, :], dsem)
                        g["kst%d" % ksti].read_by(dsem, nst)
                        g["psbb"].read_by(S("dve"), nv2)
                        g["qraw"].read_by(S("dve"), nv2)
                    return

                nwq = [wblock_load(0, WQT, 0, S("dwq0")),
                       wblock_load(1, WQT, 1, S("dwq1"))]
                k.wait("tensor", S("dxb"), n_xb)
                k.wait("tensor", S("dsm"), n_sm)
                for fc in range(NH):
                    bi = fc % 2
                    proj_norm(fc, WB[bi], g["wb%d" % bi], S("dwq%d" % bi),
                              nwq[bi], QNWSC, QSC, fc, S("dqst"))
                    if fc + 2 < NH:
                        nwq[bi] = wblock_load(bi, WQT, fc + 2, S("dwq%d" % bi))
                    qa_step()

                nwk = [wblock_load(0, WKT, 0, S("dwkv0")),
                       wblock_load(1, WKT, 1, S("dwkv1"))]
                for fc in range(NKV):
                    bi = fc % 2
                    proj_norm(fc, WB[bi], g["wb%d" % bi], S("dwkv%d" % bi),
                              nwk[bi], KNWS, K_IN, fc, S("dkvst"))
                    if fc + 2 < NKV:
                        nwk[bi] = wblock_load(bi, WKT, fc + 2, S("dwkv%d" % bi))
                    qa_step()

            with ExitStack() as ppv:
                psV = [ppv.enter_context(nc.psum_tensor("psv%d" % i, [128, 512], F32))
                       for i in range(8)]
                gv = [Res() for _ in range(8)]
                k.wait("tensor", S("act"), k.sems["act"].n)
                k.wait("tensor", S("dve"), k.sems["dve"].n)

                def vw_load(ti):
                    bi = ti % 2
                    k.guard("gpsimd", g["vw%d" % bi])
                    return k.dma("gpsimd", VW[bi][:, :],
                                 WVT[ti // NDMC,
                                     (ti % NDMC) * 128:(ti % NDMC) * 128 + 128, :],
                                 S("dwv%d" % bi))

                nvw = [vw_load(0), vw_load(1)]
                n_vst = 0
                for fh in range(2):
                    for dmc in range(NDMC):
                        ti = fh * NDMC + dmc
                        bi = ti % 2
                        k.wait("tensor", S("dwv%d" % bi), nvw[bi])
                        if dmc == 0:
                            for p in range(8):
                                k.guard("tensor", gv[p])
                        for p in range(8):
                            k.mm(psV[p][:, :],
                                 xb[:, dmc * TOK + p * 128: dmc * TOK + p * 128 + 128],
                                 VW[bi][:, :],
                                 start=(dmc == 0), stop=(dmc == NDMC - 1),
                                 sig=(p == 7))
                        npe = k.sems["pe"].n
                        g["vw%d" % bi].read_by(S("pe"), npe)
                        if ti + 2 < 2 * NDMC:
                            nvw[bi] = vw_load(ti + 2)
                    npe = k.sems["pe"].n
                    for p in range(8):
                        vi = p % 2
                        k.wait("scalar", S("pe"), npe)
                        k.guard("scalar", g["vst%d" % vi])
                        na = k.act(VST[vi][:, :], psV[p][:, :], AF.Identity)
                        gv[p].read_by(S("act"), na)
                        k.wait("sync", S("act"), na)
                        for kk in range(4):
                            row0 = (fh * 4 + kk) * TOK + p * 128
                            n_vst = k.dma("sync", V_IN[row0:row0 + 128, :],
                                          VST[vi][:, kk * 128:(kk + 1) * 128],
                                          S("dkvst"))
                        g["vst%d" % vi].read_by(S("dkvst"), n_vst)
                    qa_step()

            while qa_state[1] < len(qa_tiles):
                qa_step()

        k.wait("gpsimd", S("dkvst"), k.sems["dkvst"].n)
        k.em("gpsimd", lambda E: E.collective_compute(
            "AllGather", ALU.bypass, replica_groups=RG,
            ins=[K_IN], outs=[K_OUT]).then_inc(k.sems["cc"].h, 1))
        S("cc").inc()
        k.em("gpsimd", lambda E: E.collective_compute(
            "AllGather", ALU.bypass, replica_groups=RG,
            ins=[V_IN], outs=[V_OUT]).then_inc(k.sems["cc"].h, 1))
        n_cc_kv = S("cc").inc()

        with ExitStack() as pq2:
            psS = pq2.enter_context(nc.psum_tensor("psscal", [1, 8], F32))
            psSB = pq2.enter_context(nc.psum_tensor("psscalb", [128, 8], F32))
            nv = k.vsig()
            k.wait("tensor", S("dve"), nv)
            k.mm(psS[:, 0:3], ONESF[:, :], ABSM[:, 0:3], start=True, stop=True)
            npe = k.sems["pe"].n
            k.wait("vector", S("pe"), npe)
            k.vop(lambda E: E.memset(SCROW[:, :], 0.0))
            k.vop(lambda E: E.tensor_copy(SCROW[:, 0:3], psS[:, 0:3]))
            nv2 = k.vsig()
            k.wait("sync", S("dve"), nv2)
            nar = k.dma("sync", AR_IN, SCROW[:, :], S("dar"))
            k.wait("gpsimd", S("dar"), nar)
            k.em("gpsimd", lambda E: E.collective_compute(
                "AllReduce", ALU.add, replica_groups=RG,
                ins=[AR_IN], outs=[AR_OUT]).then_inc(k.sems["cc"].h, 1))
            n_cc_ar = S("cc").inc()
            k.wait("sync", S("cc"), n_cc_ar)
            nar2 = k.dma("sync", SCSUM[:, :], AR_OUT, S("dar"))
            k.wait("vector", S("dar"), nar2)
            spc = lambda: k.vop(lambda E: E.tensor_copy(SPC[:, :], SPC[:, :]))
            k.vop(lambda E: E.tensor_scalar(SCT[:, :], SCSUM[:, :], 1.0 / NWEL,
                                            None, ALU.mult))
            spc()
            k.vop(lambda E: E.tensor_scalar(SCT[0:1, 5:8], SCT[0:1, 0:3], EPS_BIT,
                                            None, ALU.add))
            spc()
            k.vop(lambda E: E.reciprocal(SCROW[0:1, 0:3], SCT[0:1, 5:8]))
            k.vop(lambda E: E.tensor_tensor(SCT[0:1, 4:5], SCT[0:1, 0:1],
                                            SCT[0:1, 1:2], ALU.mult))
            spc()
            k.vop(lambda E: E.tensor_tensor(SCT[0:1, 4:5], SCT[0:1, 4:5],
                                            SCT[0:1, 2:3], ALU.mult))
            nv3 = k.vsig()
            k.wait("tensor", S("dve"), nv3)
            k.mm(psSB[:, 0:4], ONEROW[:, :], SCROW[0:1, 0:4], start=True,
                 stop=True)
            npe2 = k.sems["pe"].n
            k.wait("vector", S("pe"), npe2)
            k.vop(lambda E: E.tensor_copy(INVC[:, :], psSB[:, 0:4]))
            nv4 = k.vsig()
            k.wait("tensor", S("dve"), nv4)
            k.mm(psSB[:, 0:1], ONEROW[:, :], SCT[0:1, 4:5], start=True,
                 stop=True)
            npe3 = k.sems["pe"].n
            k.wait("vector", S("pe"), npe3)
            k.vop(lambda E: E.tensor_copy(S3C[:, :], psSB[:, 0:1]))
            k.vsig()

        tap("Q", QSC)
        tap("VIN", V_IN)
        tap("ABSM", ABSM[:, 0:3])
        tap("SCSUM", SCSUM[:, :])
        tap("KAG", K_OUT)
        tap("VAG", V_OUT)
        tap("SCT", SCT[:, :])

        with ExitStack() as p2:
            sb2 = lambda n, s, d: p2.enter_context(nc.sbuf_tensor(n, s, d))
            KBUF = [sb2("kbuf%d" % i, [128, SEQ], BF16) for i in range(4)]
            CHB = [sb2("chb%d" % i, [128, 1024], F32) for i in range(4)]
            VBUF = [sb2("vbuf%d" % i, [128, SEQ], BF16) for i in range(3)]
            QT = [sb2("qt%d" % i, [128, 512], BF16) for i in range(4)]
            PRT = [sb2("prt0", [128, 512], BF16), sb2("prt1", [128, 512], BF16)]
            EXPD = sb2("expd", [128, 4], F32)
            POSHL = sb2("poshl", [2, SEQ], BF16)
            POS4 = sb2("pos4", [4, SEQ], BF16)
            SL2 = sb2("sl2", [2, SEQ], BF16)
            SLB4 = sb2("slb4", [4, SEQ], BF16)
            k.quiesce()
            n_p2sm = k.dma("sync", POSHL[:, :], POSHLD, S("dsm"))
            n_p2sm = k.dma("sync", POS4[:, :], POS4D, S("dsm"))
            n_p2sm = k.dma("sync", SL2[:, :], SL2D, S("dsm"))
            n_p2sm = k.dma("sync", SLB4[2:4, :], SL2D, S("dsm"))
            k.wait("tensor", S("dsm"), n_p2sm)
            QBL = [sb2("qbl0", [128, 640], F32), sb2("qbl1", [128, 640], F32)]
            R1B = [sb2("r1b0", [128, 640], F32), sb2("r1b1", [128, 640], F32)]
            QBO = [sb2("qbo0", [128, 640], BF16), sb2("qbo1", [128, 640], BF16)]
            ps1 = [p2.enter_context(nc.psum_tensor("ps1_%d" % i, [128, 1024], F32))
                   for i in range(2)]
            ps2 = [p2.enter_context(nc.psum_tensor("ps2_%d" % i, [128, 512], F32))
                   for i in range(2)]
            psAV = p2.enter_context(nc.psum_tensor("psav", [128, 128], F32))
            psBT = p2.enter_context(nc.psum_tensor("psbt", [2, 512], BF16))
            BCB = sb2("bcb", [128, 24], BF16)

            gk = [Res() for _ in range(4)]
            gvv = [Res(), Res(), Res()]
            gqt = [Res() for _ in range(4)]
            gprt = [Res(), Res()]
            gps1 = [Res(), Res()]
            gps2 = [Res(), Res()]
            gav = Res()
            gbt = Res()
            gbrow = Res()
            gbcb_a = [Res(), Res(), Res()]
            gbcb_v = [Res(), Res(), Res()]
            gbcol = [Res(), Res(), Res()]
            gchb = [Res(), Res(), Res(), Res()]
            gnm = Res()
            gstat = Res()

            units = [(t, kv) for t in range(NSLOT) for kv in range(NKV)]
            if DEBUG_P2_UNITS is not None:
                units = [units[i] for i in DEBUG_P2_UNITS]

            K6 = K_OUT.rearrange("(cc kv d) (tp s) -> kv cc d tp s",
                                 kv=NKV, d=128, s=128)
            V6 = V_OUT.rearrange("(cc kv tp s) d -> kv cc tp s d",
                                 kv=NKV, tp=8, s=128)
            Q6 = QSC.rearrange("(hh p) tok -> p hh tok", p=128)

            nK = [0, 0, 0, 0]
            nQ = [0, 0, 0, 0]
            nV = [0, 0, 0]

            def emit_kq_loads(u):
                t, kv = units[u]
                gg1 = (t >> 1) + 1
                b = t & 1
                par = u % 4
                k.guard("sync", gk[par])
                for gg in range(gg1):
                    tp = 2 * gg + b
                    src = K6[kv, :, :, tp, :].transpose([1, 0, 2])
                    dst = KBUF[par][:, gg * 1024:(gg + 1) * 1024].rearrange(
                        "p (cc s) -> p cc s", cc=8)
                    nK[par] = k.dma("sync", dst, src, S("dkg%d" % par))
                k.guard("sync", gqt[par])
                qsrc = Q6[:, 4 * kv:4 * kv + 4, t * 128:(t + 1) * 128]
                qdst = QT[par][:, :].rearrange("p (hh s) -> p hh s", s=128)
                nQ[par] = k.dma("sync", qdst, qsrc, S("dqt%d" % par))

            def emit_v_load(u):
                t, kv = units[u]
                gg1 = (t >> 1) + 1
                b = t & 1
                vpar = u % 3
                k.guard("sync", gvv[vpar])
                for gg in range(gg1):
                    tp = 2 * gg + b
                    vsrc = V6[kv, :, tp, :, :].transpose([1, 0, 2])
                    vdst = VBUF[vpar][:, gg * 1024:(gg + 1) * 1024].rearrange(
                        "p (cc d) -> p cc d", cc=8)
                    nV[vpar] = k.dma("sync", vdst, vsrc, S("dvg"))

            qb_tiles = []
            for m, wsrc in ((0, WG), (1, WU)):
                for rc in range(4):
                    c0 = 0
                    for w in QT_GU:
                        qb_tiles.append((m, wsrc, rc, c0, w))
                        c0 += w
            for rc in range(NCI):
                qb_tiles.append((2, WD, rc, 0, 512))
            gqbl = [Res(), Res()]
            gqbo = [Res(), Res()]
            qb_nload = {}
            qb_state = [0, 0]

            def qb_load(i):
                m, wsrc, rc, c0, w = qb_tiles[i]
                k.guard("gpsimd", gqbl[i % 2])
                qb_nload[i] = k.dma("gpsimd", QBL[i % 2][:, 0:w],
                                    wsrc[rc * 128:(rc + 1) * 128, c0:c0 + w],
                                    S("dqb%d" % (i % 2)))

            def qb_proc(i):
                m, wsrc, rc, c0, w = qb_tiles[i]
                pi = i % 2
                k.wait("gpsimd", S("dqb%d" % pi), qb_nload[i])
                k.em("gpsimd", lambda E, pi=pi, w=w, m=m: E.tensor_scalar(
                    R1B[pi][:, 0:w], QBL[pi][:, 0:w], INVC[:, m:m + 1], MAGIC,
                    ALU.mult, ALU.add))
                k.em("gpsimd", lambda E, pi=pi, w=w: E.tensor_scalar(
                    R1B[pi][:, 0:w], R1B[pi][:, 0:w], -MAGIC, 1.0,
                    ALU.add, ALU.min))
                k.guard("gpsimd", gqbo[pi])
                k.em("gpsimd", lambda E, pi=pi, w=w: E.tensor_scalar(
                    QBO[pi][:, 0:w], R1B[pi][:, 0:w], -1.0, None, ALU.max))
                if m < 2:
                    base = m * 4 * IFD + rc * IFD + c0
                    nci_t = w // 128
                    dst = FGU_IN[base:base + w, :].rearrange(
                        "(ci p) f -> p ci f", p=128)
                    src = QBO[pi][:, 0:w].rearrange("p (ci f) -> p ci f", f=128)
                    nw_ = k.dma("gpsimd", dst, src, S("dffnst"))
                else:
                    for dd in range(4):
                        dst = FD_IN[dd * IFD + rc * 128: dd * IFD + rc * 128 + 128, :]
                        nw_ = k.dma("gpsimd", dst,
                                    QBO[pi][:, dd * 128:dd * 128 + 128],
                                    S("dffnst"))
                gqbo[pi].read_by(S("dffnst"), nw_)

            def qb_step(target):
                while qb_state[1] < min(target, len(qb_tiles)):
                    if qb_state[0] < min(qb_state[1] + 2, len(qb_tiles)):
                        qb_load(qb_state[0])
                        qb_state[0] += 1
                    if qb_state[0] >= qb_state[1] + 1:
                        qb_proc(qb_state[1])
                        qb_state[1] += 1

            qb_load(0)
            qb_load(1)
            qb_state[0] = 2
            qb_ag_done = [False]

            k.wait("sync", S("cc"), n_cc_kv)
            k.wait("sync", S("dqst"), k.sems["dqst"].n)
            emit_kq_loads(0)

            ustate = {}

            def emit_pass1(u):
                t, kv = units[u]
                gg1 = (t >> 1) + 1
                par = u % 4
                vpar = u % 3
                boff = (u % 3) * 4
                bo2 = (u % 3) * 8
                nKu, nQu = nK[par], nQ[par]
                emit_v_load(u)
                nVu = nV[vpar]
                if u + 1 < len(units):
                    emit_kq_loads(u + 1)

                k.wait("tensor", S("dkg%d" % par), nKu)
                k.wait("tensor", S("dqt%d" % par), nQu)

                for hh in range(4):
                    h = 4 * kv + hh
                    c0 = hh * 4
                    for n in range(gg1):
                        c1 = n % 2
                        u0 = n * 1024
                        k.guard("tensor", gps1[c1])
                        for q2 in range(2):
                            psl = ps1[c1][:, q2 * 512:q2 * 512 + 512]
                            cb = u0 + q2 * 512
                            k.mm(psl, QT[par][:, hh * 128:hh * 128 + 128],
                                 KBUF[par][:, cb:cb + 512],
                                 start=True, stop=False)
                            last = (n != gg1 - 1)
                            k.mm(psl, SL2[0:2, h * 128:h * 128 + 128],
                                 POSHL[0:2, cb:cb + 512],
                                 start=False, stop=last)
                            if not last:
                                k.mm(psl, GE01[:, :],
                                     MSK1S[:, q2 * 512:q2 * 512 + 512],
                                     start=False, stop=True)
                        npe = k.sems["pe"].n
                        k.wait("scalar", S("pe"), npe)
                        k.guard("scalar", gchb[n])
                        nev = k.act(CHB[n][:, :], ps1[c1][:, :], AF.Identity)
                        gps1[c1].read_by(S("act"), nev)
                    k.wait("vector", S("act"), k.sems["act"].n)
                    k.guard("vector", gnm)
                    for n in range(gg1):
                        k.vop(lambda E, n=n, col=c0 + n: E.tensor_reduce(
                            NM[:, col:col + 1], CHB[n][:, :], AX.X, ALU.max,
                            negate=True))
                    k.vop(lambda E: E.tensor_copy(SPC[:, :], SPC[:, :]))
                    k.vop(lambda E, c0=c0, C=gg1, hh=hh: E.tensor_reduce(
                        NMA[:, hh:hh + 1], NM[:, c0:c0 + C], AX.X, ALU.min))
                    nvm = k.vsig()
                    k.wait("scalar", S("dve"), nvm)
                    if hh == 0:
                        k.guard("scalar", gbcol[u % 3])
                        k.guard("scalar", gbcb_a[u % 3])
                    for n in range(gg1):
                        nex = k.act(DUMP[:, :], CHB[n][:, :], AF.Exp,
                                    bias=NMA[:, hh:hh + 1],
                                    accum_out=EBF[:, c0 + n:c0 + n + 1])
                        gchb[n].read_by(S("act"), nex)
                    k.act(DUMP[:, 0:gg1], EBF[:, c0:c0 + gg1], AF.Identity,
                          accum_out=SIGB[:, hh:hh + 1])
                    k.act(LNSB[:, hh:hh + 1], SIGB[:, hh:hh + 1], AF.Ln)
                    k.act(BCOL[:, boff + hh:boff + hh + 1], LNSB[:, hh:hh + 1],
                          AF.Identity, scale=-1.0, bias=NMA[:, hh:hh + 1])
                    k.act(BCB[:, bo2 + 2 * hh:bo2 + 2 * hh + 1],
                          LNSB[:, hh:hh + 1], AF.Identity, scale=-1.0,
                          bias=NMA[:, hh:hh + 1])
                gnm.read_by(S("dve"), nvm)
                na_tail = k.sems["act"].n
                ustate[u] = (par, vpar, boff, bo2, nVu, na_tail)

            def emit_pass2(u):
                t, kv = units[u]
                gg1 = (t >> 1) + 1
                nblk = gg1 * 8
                par, vpar, boff, bo2, nVu, na_tail = ustate.pop(u)

                k.wait("vector", S("act"), na_tail)
                k.guard("vector", gbcb_v[u % 3])
                for hh in range(4):
                    c = boff + hh
                    k.vop(lambda E, hh=hh, c=c, bo2=bo2: E.tensor_tensor(
                        BCB[:, bo2 + 2 * hh + 1:bo2 + 2 * hh + 2],
                        BCOL[:, c:c + 1],
                        BCB[:, bo2 + 2 * hh:bo2 + 2 * hh + 1], ALU.subtract))
                    k.vop(lambda E: E.tensor_copy(SPC[:, :], SPC[:, :]))
                nvbc = k.vsig()
                gbcol[u % 3].read_by(S("dve"), nvbc)
                k.wait("tensor", S("dve"), nvbc)
                k.guard("tensor", gbt)
                npbt = 0
                for hh in range(4):
                    npbt = k.tr(psBT[0:2, hh * 128:hh * 128 + 128],
                                BCB[:, bo2 + 2 * hh:bo2 + 2 * hh + 2], IDB[:, :])
                gbcb_a[u % 3].read_by(S("pe"), npbt)
                gbcb_v[u % 3].read_by(S("pe"), npbt)
                k.wait("vector", S("pe"), npbt)
                k.guard("vector", gbrow)
                for hh in range(4):
                    h = 4 * kv + hh
                    k.vop(lambda E, hh=hh, h=h: E.tensor_copy(
                        SLB4[0:2, h * 128:h * 128 + 128],
                        psBT[0:2, hh * 128:hh * 128 + 128]))
                nvb2 = k.vsig()
                gbt.read_by(S("dve"), nvb2)

                k.wait("tensor", S("dve"), nvb2)
                k.wait("tensor", S("dvg"), nVu)
                for hh in range(4):
                    h = 4 * kv + hh
                    for grp in range(nblk // 4):
                        p2i = grp % 2
                        k.guard("tensor", gps2[p2i])
                        k.guard("tensor", gprt[p2i])
                        for jj in range(4):
                            j = grp * 4 + jj
                            psl = ps2[p2i][:, jj * 128:jj * 128 + 128]
                            k.mm(psl, KBUF[par][:, j * 128:j * 128 + 128],
                                 QT[par][:, hh * 128:hh * 128 + 128],
                                 start=True, stop=False)
                            masked = (j >= nblk - 8)
                            k.mm(psl, POS4[0:4, j * 128:j * 128 + 128],
                                 SLB4[0:4, h * 128:h * 128 + 128],
                                 start=False, stop=(not masked))
                            if masked:
                                w = j - (nblk - 8)
                                k.mm(psl, MSK2S[:, w * 128:w * 128 + 128],
                                     GE01[:, :], start=False, stop=True)
                        npe2 = k.sems["pe"].n
                        k.wait("scalar", S("pe"), npe2)
                        nexp = k.act(PRT[p2i][:, :], ps2[p2i][:, :], AF.Exp)
                        gps2[p2i].read_by(S("act"), nexp)
                        k.wait("tensor", S("act"), nexp)
                        if grp == 0:
                            k.guard("tensor", gav)
                        for jj in range(4):
                            j = grp * 4 + jj
                            k.mm(psAV[:, :], VBUF[vpar][:, j * 128:j * 128 + 128],
                                 PRT[p2i][:, jj * 128:jj * 128 + 128],
                                 start=(j == 0), stop=(j == nblk - 1),
                                 sig=(jj == 3))
                        gprt[p2i].read_by(S("pe"), k.sems["pe"].n)
                    npe3 = k.sems["pe"].n
                    k.wait("scalar", S("pe"), npe3)
                    nctx = k.act(AO[:, h * TOK + t * 128: h * TOK + t * 128 + 128],
                                 psAV[:, :], AF.Identity)
                    gav.read_by(S("act"), nctx)
                npe_end = k.sems["pe"].n
                gk[par].read_by(S("pe"), npe_end)
                gqt[par].read_by(S("pe"), npe_end)
                gvv[vpar].read_by(S("pe"), npe_end)
                gbrow.read_by(S("pe"), npe_end)

            pe_mark = {}
            for u in range(len(units)):
                if u > 1:
                    emit_pass2(u - 2)
                    pe_mark[u - 2] = k.sems["pe"].n
                emit_pass1(u)
                qb_step((u + 1) * len(qb_tiles) // 48)
            for u in (len(units) - 2, len(units) - 1):
                if u in ustate:
                    emit_pass2(u)
                    pe_mark[u] = k.sems["pe"].n
            assert qb_state[1] >= len(qb_tiles), "quant-B did not finish in P2"
            u1 = min(50, len(units) - 1)
            u2 = min(55, len(units) - 1)
            k.wait("gpsimd", S("dffnst"), k.sems["dffnst"].n)
            k.wait("gpsimd", S("pe"), pe_mark[u1])
            k.em("gpsimd", lambda E: E.collective_compute(
                "AllGather", ALU.bypass, replica_groups=RG,
                ins=[FGU_IN], outs=[FGU_OUT]).then_inc(k.sems["cc"].h, 1))
            n_cc_fgu = S("cc").inc()
            k.wait("gpsimd", S("pe"), pe_mark[u2])
            k.em("gpsimd", lambda E: E.collective_compute(
                "AllGather", ALU.bypass, replica_groups=RG,
                ins=[FD_IN], outs=[FD_OUT]).then_inc(k.sems["cc"].h, 1))
            S("cc").inc()
            n_cc_all = k.sems["cc"].n
            n_act_ao = k.sems["act"].n

        tap("AO", AO[:, :])
        tap("NM", NM[:, :])
        tap("EBF", EBF[:, :])
        tap("NMA", NMA[:, :])
        tap("SIGB", SIGB[:, :])
        tap("LNSB", LNSB[:, :])
        tap("BCOL", BCOL[:, :])
        tap("BROW4", BROW4[:, :])

        with ExitStack() as p3:
            sb3 = lambda n, ss, d: p3.enter_context(nc.sbuf_tensor(n, ss, d))
            WOB = [sb3("wob0", [128, NDMC * 128], BF16),
                   sb3("wob1", [128, NDMC * 128], BF16)]
            XTC = [sb3("xtc0", [128, TOK], F32), sb3("xtc1", [128, TOK], F32)]
            X1B = [sb3("x1b0", [128, TOK], F32), sb3("x1b1", [128, TOK], F32)]
            X1SQ = sb3("x1sq", [128, TOK], BF16)
            MROW = sb3("mrow", [1, TOK], F32)
            M2ROW = sb3("m2row", [1, TOK], F32)
            VROW = sb3("vrow", [1, TOK], F32)
            SROW3 = sb3("srow3", [1, TOK], F32)
            IROW3 = sb3("irow3", [1, TOK], F32)
            psO = [p3.enter_context(nc.psum_tensor("pso%d" % i, [128, 512], F32))
                   for i in range(2)]
            psS1 = p3.enter_context(nc.psum_tensor("pss1", [1, TOK], F32))
            psS2 = p3.enter_context(nc.psum_tensor("pss2", [1, TOK], F32))
            psMB = p3.enter_context(nc.psum_tensor("psmb", [128, TOK], F32))

            gwo = [Res(), Res()]
            gxtc = [Res(), Res()]
            gx1b = [Res(), Res()]
            gx1sq = Res()
            gpso = [Res(), Res()]

            def wob_load(dmc):
                bi = dmc % 2
                k.guard("gpsimd", gwo[bi])
                src = WOT[dmc].rearrange("(a p) f -> p a f", p=128)
                dst = WOB[bi][:, :].rearrange("p (a f) -> p a f", f=128)
                return k.dma("gpsimd", dst, src, S("dwo%d" % bi))

            def xtc_load(dmc):
                bi = dmc % 2
                k.guard("sync", gxtc[bi])
                return k.dma("sync", XTC[bi][:, :],
                             XT[dmc * 128:(dmc + 1) * 128, :], S("dxt%d" % bi))

            k.quiesce()
            nwo = [wob_load(0), wob_load(1)]
            nxt = [xtc_load(0), xtc_load(1)]
            n_x1w = 0
            for dmc in range(NDMC):
                bi = dmc % 2
                k.wait("tensor", S("dwo%d" % bi), nwo[bi])
                nvx = 0
                for half in range(2):
                    k.guard("tensor", gpso[half])
                    for fc in range(NH):
                        k.mm(psO[half][:, :], WOB[bi][:, fc * 128:(fc + 1) * 128],
                             AO[:, fc * TOK + half * 512: fc * TOK + half * 512 + 512],
                             start=(fc == 0), stop=(fc == NH - 1))
                npe = k.sems["pe"].n
                gwo[bi].read_by(S("pe"), npe)
                k.wait("vector", S("pe"), npe)
                k.wait("vector", S("dxt%d" % bi), nxt[bi])
                k.guard("vector", gx1b[bi])
                for half in range(2):
                    k.vop(lambda E, bi=bi, half=half: E.tensor_tensor(
                        X1B[bi][:, half * 512:half * 512 + 512], psO[half][:, :],
                        XTC[bi][:, half * 512:half * 512 + 512], ALU.add))
                nvx = k.vsig()
                for half in range(2):
                    gpso[half].read_by(S("dve"), nvx)
                gxtc[bi].read_by(S("dve"), nvx)
                k.wait("tensor", S("dve"), nvx)
                for half in range(2):
                    k.mm(psS1[0:1, half * 512:half * 512 + 512], ONESF[:, :],
                         X1B[bi][:, half * 512:half * 512 + 512],
                         start=(dmc == 0), stop=(dmc == NDMC - 1), sig=True)
                k.wait("scalar", S("dve"), nvx)
                k.guard("scalar", gx1sq)
                nsq = k.act(X1SQ[:, :], X1B[bi][:, :], AF.Square)
                k.wait("tensor", S("act"), nsq)
                for half in range(2):
                    k.mm(psS2[0:1, half * 512:half * 512 + 512], ONESB[:, :],
                         X1SQ[:, half * 512:half * 512 + 512],
                         start=(dmc == 0), stop=(dmc == NDMC - 1), sig=True)
                npe2 = k.sems["pe"].n
                gx1sq.read_by(S("pe"), npe2)
                k.wait("sync", S("dve"), nvx)
                n_x1w = k.dma("sync", X1SC[dmc * 128:(dmc + 1) * 128, :],
                              X1B[bi][:, :], S("dx1w"))
                gx1b[bi].read_by(S("pe"), npe2)
                gx1b[bi].read_by(S("dx1w"), n_x1w)
                if dmc + 2 < NDMC:
                    nwo[bi] = wob_load(dmc + 2)
                    nxt[bi] = xtc_load(dmc + 2)
            npe3 = k.sems["pe"].n
            k.wait("scalar", S("pe"), npe3)
            k.act(MROW[:, :], psS1[:, :], AF.Identity, scale=1.0 / DM)
            k.act(M2ROW[:, :], psS1[:, :], AF.Square, scale=1.0 / DM)
            na3 = k.act(VROW[:, :], psS2[:, :], AF.Identity, scale=1.0 / DM)
            k.wait("vector", S("act"), na3)
            k.vop(lambda E: E.tensor_tensor(VROW[:, :], VROW[:, :], M2ROW[:, :],
                                            ALU.subtract))
            nv5 = k.vsig()
            k.wait("scalar", S("dve"), nv5)
            na4 = k.act(SROW3[:, :], VROW[:, :], AF.Sqrt, bias=EPS_LN)
            k.wait("vector", S("act"), na4)
            k.vop(lambda E: E.reciprocal(IROW3[:, :], SROW3[:, :]))
            nv6 = k.vsig()
            k.wait("tensor", S("dve"), nv6)
            for half in range(2):
                k.mm(psMB[:, half * 512:half * 512 + 512], ONEROW[:, :],
                     MROW[0:1, half * 512:half * 512 + 512], start=True, stop=True)
            npe4 = k.sems["pe"].n
            k.wait("scalar", S("pe"), npe4)
            k.guard("scalar", gmbb)
            na5 = k.act(MBB[:, :], psMB[:, :], AF.Identity)
            k.wait("tensor", S("act"), na5)
            for half in range(2):
                k.mm(psMB[:, half * 512:half * 512 + 512], ONEROW[:, :],
                     IROW3[0:1, half * 512:half * 512 + 512], start=True, stop=True)
            npe5 = k.sems["pe"].n
            k.wait("scalar", S("pe"), npe5)
            na6 = k.act(IBB[:, :], psMB[:, :], AF.Identity)

        ao_stack.close()

        xb1_stack = ExitStack()
        xb1 = xb1_stack.enter_context(
            nc.sbuf_tensor("xb1", [128, NDMC * TOK], BF16))
        n_xb1 = 0
        n_x1nw = 0
        with ExitStack() as p3b:
            sb3b = lambda n, ss, d: p3b.enter_context(nc.sbuf_tensor(n, ss, d))
            X1R = [sb3b("x1r0", [128, TOK], F32), sb3b("x1r1", [128, TOK], F32)]
            X1NB = [sb3b("x1nb0", [128, TOK], F32), sb3b("x1nb1", [128, TOK], F32)]
            gx1r = [Res(), Res()]
            gx1n = [Res(), Res()]
            nrd = [0, 0]
            k.quiesce()
            k.wait("sync", S("dx1w"), k.sems["dx1w"].n)
            k.wait("sync", S("act"), na6)
            nrd[0] = k.dma("sync", X1R[0][:, :], X1SC[0:128, :], S("dx1r0"))
            nrd[1] = k.dma("sync", X1R[1][:, :], X1SC[128:256, :], S("dx1r1"))
            for dmc in range(NDMC):
                bi = dmc % 2
                k.wait("vector", S("dx1r%d" % bi), nrd[bi])
                k.wait("vector", S("act"), na6)
                k.guard("vector", gx1n[bi])
                k.vop(lambda E, bi=bi: E.tensor_tensor(
                    X1NB[bi][:, :], X1R[bi][:, :], MBB[:, :], ALU.subtract))
                k.vop(lambda E, bi=bi: E.tensor_tensor(
                    X1NB[bi][:, :], X1NB[bi][:, :], IBB[:, :], ALU.mult))
                k.vop(lambda E, bi=bi, dmc=dmc: E.tensor_copy(
                    xb1[:, dmc * TOK:(dmc + 1) * TOK], X1NB[bi][:, :]))
                nvr = k.vsig()
                gx1r[bi].read_by(S("dve"), nvr)
                if dmc == NDMC - 1:
                    gmbb.read_by(S("dve"), nvr)
                k.wait("sync", S("dve"), nvr)
                n_x1nw = k.dma("sync", X1N[dmc * 128:(dmc + 1) * 128, :],
                               X1NB[bi][:, :], S("dx1nw"))
                gx1n[bi].read_by(S("dx1nw"), n_x1nw)
                if dmc + 2 < NDMC:
                    k.guard("sync", gx1r[bi])
                    nrd[bi] = k.dma("sync", X1R[bi][:, :],
                                    X1SC[(dmc + 2) * 128:(dmc + 3) * 128, :],
                                    S("dx1r%d" % bi))
            n_xb1 = k.sems["dve"].n

        with ExitStack() as p4:
            sb4 = lambda n, ss, d: p4.enter_context(nc.sbuf_tensor(n, ss, d))
            GB = [sb4("gb0", [128, NDMC * 128], BF16),
                  sb4("gb1", [128, NDMC * 128], BF16)]
            UB = [sb4("ub0", [128, NDMC * 128], BF16),
                  sb4("ub1", [128, NDMC * 128], BF16)]
            GF = [sb4("gf0", [128, 512], F32), sb4("gf1", [128, 512], F32)]
            INTB = [sb4("intb0", [128, 512], BF16), sb4("intb1", [128, 512], BF16)]
            psG = [p4.enter_context(nc.psum_tensor("psg%d" % i, [128, 512], F32))
                   for i in range(2)]
            psU = [p4.enter_context(nc.psum_tensor("psu%d" % i, [128, 512], F32))
                   for i in range(2)]
            ggb = [Res(), Res()]
            gub = [Res(), Res()]
            ggf = [Res(), Res()]
            gintb = [Res(), Res()]
            gpsg = [Res(), Res()]
            gpsu = [Res(), Res()]
            FGU6 = FGU_OUT.rearrange("(r sec pc ci p) f -> r sec pc ci p f",
                                     r=NCORE, sec=2, pc=4, p=128)

            def gu_load(ci, which, bi):
                gr = (ggb if which == 0 else gub)[bi]
                k.guard("sync", gr)
                buf = (GB if which == 0 else UB)[bi]
                nn_ = 0
                for r in range(NCORE):
                    src = FGU6[r, which, :, ci, :, :]
                    dst = buf[:, r * 512:(r + 1) * 512].rearrange(
                        "p (pc f) -> p pc f", f=128)
                    nn_ = k.dma("sync", dst, src.transpose([1, 0, 2]),
                                S("dwg%d" % bi))
                return nn_

            k.quiesce()
            k.wait("sync", S("cc"), n_cc_fgu)
            ngb = [gu_load(0, 0, 0), 0]
            nub = [gu_load(0, 1, 0), 0]
            ngb[1] = gu_load(1, 0, 1)
            nub[1] = gu_load(1, 1, 1)
            k.wait("tensor", S("dve"), n_xb1)
            n_intw = 0
            for ci in range(NCI):
                bi = ci % 2
                k.wait("tensor", S("dwg%d" % bi), max(ngb[bi], nub[bi]))
                for half in range(2):
                    par = (ci * 2 + half) % 2
                    k.guard("tensor", gpsg[par])
                    k.guard("tensor", gpsu[par])
                    for dmc in range(NDMC):
                        k.mm(psG[par][:, :], GB[bi][:, dmc * 128:(dmc + 1) * 128],
                             xb1[:, dmc * TOK + half * 512: dmc * TOK + half * 512 + 512],
                             start=(dmc == 0), stop=(dmc == NDMC - 1))
                    for dmc in range(NDMC):
                        k.mm(psU[par][:, :], UB[bi][:, dmc * 128:(dmc + 1) * 128],
                             xb1[:, dmc * TOK + half * 512: dmc * TOK + half * 512 + 512],
                             start=(dmc == 0), stop=(dmc == NDMC - 1))
                    npe = k.sems["pe"].n
                    k.wait("scalar", S("pe"), npe)
                    k.guard("scalar", ggf[par])
                    ngf = k.act(GF[par][:, :], psG[par][:, :], AF.Identity)
                    gpsg[par].read_by(S("act"), ngf)
                    k.wait("vector", S("act"), ngf)
                    k.guard("vector", gintb[par])
                    k.vop(lambda E, par=par: E.tensor_tensor(
                        INTB[par][:, :], psU[par][:, :], GF[par][:, :], ALU.mult))
                    nvi = k.vsig()
                    gpsu[par].read_by(S("dve"), nvi)
                    ggf[par].read_by(S("dve"), nvi)
                    k.wait("sync", S("dve"), nvi)
                    n_intw = k.dma("sync",
                                   INTER[ci * 128:(ci + 1) * 128,
                                         half * 512:half * 512 + 512],
                                   INTB[par][:, :], S("dintw"))
                    gintb[par].read_by(S("dintw"), n_intw)
                npe_ci = k.sems["pe"].n
                ggb[bi].read_by(S("pe"), npe_ci)
                gub[bi].read_by(S("pe"), npe_ci)
                if ci + 2 < NCI:
                    ngb[bi] = gu_load(ci + 2, 0, bi)
                    nub[bi] = gu_load(ci + 2, 1, bi)
        xb1_stack.close()

        with ExitStack() as p5:
            sb5 = lambda n, ss, d: p5.enter_context(nc.sbuf_tensor(n, ss, d))
            INTH = sb5("inth", [128, NCI * 512], BF16)
            WDS = [sb5("wds0", [128, NCI * 128], BF16),
                   sb5("wds1", [128, NCI * 128], BF16)]
            XNC = [sb5("xnc0", [128, 512], F32), sb5("xnc1", [128, 512], F32)]
            DF = [sb5("df0", [128, 512], F32), sb5("df1", [128, 512], F32)]
            X2B = [sb5("x2b0", [128, 512], F32), sb5("x2b1", [128, 512], F32)]
            X2SQ = sb5("x2sq", [128, 512], BF16)
            X2R = [sb5("x2r0", [128, 512], F32), sb5("x2r1", [128, 512], F32)]
            OUTB = [sb5("outb0", [128, 512], F32), sb5("outb1", [128, 512], F32)]
            MR5 = sb5("mr5", [1, 512], F32)
            M25 = sb5("m25", [1, 512], F32)
            VR5 = sb5("vr5", [1, 512], F32)
            SR5 = sb5("sr5", [1, 512], F32)
            IR5 = sb5("ir5", [1, 512], F32)
            psD = [p5.enter_context(nc.psum_tensor("psd%d" % i, [128, 512], F32))
                   for i in range(2)]
            psS1b = p5.enter_context(nc.psum_tensor("pss1b", [1, 512], F32))
            psS2b = p5.enter_context(nc.psum_tensor("pss2b", [1, 512], F32))
            psMB5 = p5.enter_context(nc.psum_tensor("psmb5", [128, 512], F32))

            gwds = [Res(), Res()]
            gxnc = [Res(), Res()]
            gdf = [Res(), Res()]
            gx2b = [Res(), Res()]
            gx2sq = Res()
            gx2r = [Res(), Res()]
            goutb = [Res(), Res()]
            gpsd = [Res(), Res()]
            ginth = Res()

            def wds_load(dmc):
                bi = dmc % 2
                r, dd = dmc >> 2, dmc & 3
                k.guard("sync", gwds[bi])
                src = FD_OUT[r * 4 * IFD + dd * IFD: r * 4 * IFD + (dd + 1) * IFD,
                             :].rearrange("(ci p) f -> p ci f", p=128)
                dst = WDS[bi][:, :].rearrange("p (ci f) -> p ci f", f=128)
                return k.dma("sync", dst, src, S("dwd%d" % bi))

            n_out = 0
            k.quiesce()
            k.wait("sync", S("cc"), n_cc_all)
            for tp in range(2):
                k.guard("sync", ginth)
                k.wait("sync", S("dintw"), k.sems["dintw"].n)
                nih = k.dma("sync", INTH[:, :].rearrange("p (ci t) -> p ci t", t=512),
                            INTER.rearrange("(ci p) t -> p ci t", p=128)[
                                :, :, tp * 512:tp * 512 + 512], S("dintr0"))
                nwd = [wds_load(0), wds_load(1)]
                nxn = [0, 0]
                k.wait("sync", S("dx1nw"), k.sems["dx1nw"].n)
                nxn[0] = k.dma("sync", XNC[0][:, :],
                               X1N[0:128, tp * 512:tp * 512 + 512], S("dxn0"))
                nxn[1] = k.dma("sync", XNC[1][:, :],
                               X1N[128:256, tp * 512:tp * 512 + 512], S("dxn1"))
                k.wait("tensor", S("dintr0"), nih)
                n_x2w = 0
                for dmc in range(NDMC):
                    bi = dmc % 2
                    k.wait("tensor", S("dwd%d" % bi), nwd[bi])
                    k.guard("tensor", gpsd[bi])
                    for ci in range(NCI):
                        k.mm(psD[bi][:, :], WDS[bi][:, ci * 128:(ci + 1) * 128],
                             INTH[:, ci * 512:(ci + 1) * 512],
                             start=(ci == 0), stop=(ci == NCI - 1))
                    npe = k.sems["pe"].n
                    gwds[bi].read_by(S("pe"), npe)
                    k.wait("scalar", S("pe"), npe)
                    k.guard("scalar", gdf[bi])
                    ndf = k.act(DF[bi][:, :], psD[bi][:, :], AF.Identity,
                                scale=S3C[:, :])
                    gpsd[bi].read_by(S("act"), ndf)
                    k.wait("vector", S("act"), ndf)
                    k.wait("vector", S("dxn%d" % bi), nxn[bi])
                    k.guard("vector", gx2b[bi])
                    k.vop(lambda E, bi=bi: E.tensor_tensor(
                        X2B[bi][:, :], DF[bi][:, :], XNC[bi][:, :], ALU.add))
                    nvx2 = k.vsig()
                    gdf[bi].read_by(S("dve"), nvx2)
                    gxnc[bi].read_by(S("dve"), nvx2)
                    k.wait("tensor", S("dve"), nvx2)
                    k.mm(psS1b[:, :], ONESF[:, :], X2B[bi][:, :],
                         start=(dmc == 0), stop=(dmc == NDMC - 1), sig=True)
                    k.wait("scalar", S("dve"), nvx2)
                    k.guard("scalar", gx2sq)
                    nsq2 = k.act(X2SQ[:, :], X2B[bi][:, :], AF.Square)
                    k.wait("tensor", S("act"), nsq2)
                    k.mm(psS2b[:, :], ONESB[:, :], X2SQ[:, :],
                         start=(dmc == 0), stop=(dmc == NDMC - 1), sig=True)
                    npe2 = k.sems["pe"].n
                    gx2sq.read_by(S("pe"), npe2)
                    k.wait("sync", S("dve"), nvx2)
                    n_x2w = k.dma("sync",
                                  X2SC[dmc * 128:(dmc + 1) * 128,
                                       tp * 512:tp * 512 + 512],
                                  X2B[bi][:, :], S("dx2w"))
                    gx2b[bi].read_by(S("pe"), npe2)
                    gx2b[bi].read_by(S("dx2w"), n_x2w)
                    if dmc + 2 < NDMC:
                        nwd[bi] = wds_load(dmc + 2)
                        k.guard("sync", gxnc[bi])
                        nxn[bi] = k.dma("sync", XNC[bi][:, :],
                                        X1N[(dmc + 2) * 128:(dmc + 3) * 128,
                                            tp * 512:tp * 512 + 512],
                                        S("dxn%d" % bi))
                ginth.read_by(S("pe"), k.sems["pe"].n)
                npe3 = k.sems["pe"].n
                k.wait("scalar", S("pe"), npe3)
                k.act(MR5[:, :], psS1b[:, :], AF.Identity, scale=1.0 / DM)
                k.act(M25[:, :], psS1b[:, :], AF.Square, scale=1.0 / DM)
                na3 = k.act(VR5[:, :], psS2b[:, :], AF.Identity, scale=1.0 / DM)
                k.wait("vector", S("act"), na3)
                k.vop(lambda E: E.tensor_tensor(VR5[:, :], VR5[:, :], M25[:, :],
                                                ALU.subtract))
                nv5 = k.vsig()
                k.wait("scalar", S("dve"), nv5)
                na4 = k.act(SR5[:, :], VR5[:, :], AF.Sqrt, bias=EPS_LN)
                k.wait("vector", S("act"), na4)
                k.vop(lambda E: E.reciprocal(IR5[:, :], SR5[:, :]))
                nv6 = k.vsig()
                k.wait("tensor", S("dve"), nv6)
                k.mm(psMB5[:, :], ONEROW[:, :], MR5[:, :], start=True, stop=True)
                npe4 = k.sems["pe"].n
                k.wait("scalar", S("pe"), npe4)
                k.guard("scalar", gmbb)
                na5 = k.act(MBB[:, 0:512], psMB5[:, :], AF.Identity)
                k.wait("tensor", S("act"), na5)
                k.mm(psMB5[:, :], ONEROW[:, :], IR5[:, :], start=True, stop=True)
                npe5 = k.sems["pe"].n
                k.wait("scalar", S("pe"), npe5)
                na6b = k.act(IBB[:, 0:512], psMB5[:, :], AF.Identity)
                nr2 = [0, 0]
                k.wait("sync", S("dx2w"), k.sems["dx2w"].n)
                k.guard("sync", gx2r[0])
                nr2[0] = k.dma("sync", X2R[0][:, :],
                               X2SC[0:128, tp * 512:tp * 512 + 512], S("dx2r0"))
                k.guard("sync", gx2r[1])
                nr2[1] = k.dma("sync", X2R[1][:, :],
                               X2SC[128:256, tp * 512:tp * 512 + 512], S("dx2r1"))
                for dmc in range(NDMC):
                    bi = dmc % 2
                    k.wait("vector", S("dx2r%d" % bi), nr2[bi])
                    k.wait("vector", S("act"), na6b)
                    k.guard("vector", goutb[bi])
                    k.vop(lambda E, bi=bi: E.tensor_tensor(
                        OUTB[bi][:, :], X2R[bi][:, :], MBB[:, 0:512], ALU.subtract))
                    k.vop(lambda E, bi=bi: E.tensor_tensor(
                        OUTB[bi][:, :], OUTB[bi][:, :], IBB[:, 0:512], ALU.mult))
                    nvo = k.vsig()
                    gx2r[bi].read_by(S("dve"), nvo)
                    if dmc == NDMC - 1:
                        gmbb.read_by(S("dve"), nvo)
                    k.wait("sync", S("dve"), nvo)
                    n_out = k.dma("sync",
                                  OUT[dmc * 128:(dmc + 1) * 128,
                                      tp * 512:tp * 512 + 512],
                                  OUTB[bi][:, :], S("dout"))
                    goutb[bi].read_by(S("dout"), n_out)
                    if dmc + 2 < NDMC:
                        k.guard("sync", gx2r[bi])
                        nr2[bi] = k.dma("sync", X2R[bi][:, :],
                                        X2SC[(dmc + 2) * 128:(dmc + 3) * 128,
                                             tp * 512:tp * 512 + 512],
                                        S("dx2r%d" % bi))
                nv_fin = k.sems["dve"].n

        tap("X1N", X1N)
        tap("INVC", INVC[:, 0:3])
        tap("FDIN", FD_IN)
        tap("FGUIN", FGU_IN)
        tap("X1SC", X1SC)
        tap("INTER", INTER)
        tap("X2SC", X2SC)
        with ExitStack() as pz:
            k.wait("sync", S("cc"), k.sems["cc"].n)
            k.wait("gpsimd", S("cc"), k.sems["cc"].n)
            k.wait("gpsimd", S("dve"), k.sems["dve"].n)
            k.wait("gpsimd", S("act"), k.sems["act"].n)
            k.wait("sync", S("act"), k.sems["act"].n)
            k.wait("sync", S("dve"), k.sems["dve"].n)
            k.wait("gpsimd", S("dqst"), k.sems["dqst"].n)
            for (t, srcap) in tap_list:
                q = "sync" if srcap.dtype == F32 else "gpsimd"
                k.dma(q, t, srcap, S("dout"))
            k.wait("sync", S("dout"), k.sems["dout"].n)

        with ExitStack() as sc:
            for s in k.sems.values():
                s.h = sc.enter_context(nc.semaphore(s.name))
            block = sc.enter_context(nc.Block())
            k.emit(block)

    return nc


def _prep_shared(Wq, Wk, Wv, Wo, w_gate, w_up, w_down):
    d = {}
    d["WQT"] = np.ascontiguousarray(Wq.T.reshape(DM, NH, HD).transpose(1, 0, 2))
    d["WKT"] = np.ascontiguousarray(Wk.T.reshape(DM, NKV, HD).transpose(1, 0, 2))
    d["WVT"] = np.ascontiguousarray(Wv.T.reshape(DM, 2, 512).transpose(1, 0, 2))
    d["WOT"] = np.ascontiguousarray(
        np.ascontiguousarray(Wo.T).reshape(DM, NDMC, 128).transpose(1, 0, 2))
    d["WGT_full"] = np.ascontiguousarray(w_gate.T)
    d["WUT_full"] = np.ascontiguousarray(w_up.T)
    d["WDT_full"] = np.ascontiguousarray(w_down.T)
    return d


def _masks_for_core(c):
    m1 = np.zeros((128, 1024), dtype=np.float32)
    for w in range(8):
        if w > c:
            m1[127, w * 128:(w + 1) * 128] = NEGBIG
        elif w == c:
            for u in range(1, 128):
                m1[u - 1, w * 128 + u] = NEGBIG
    m2 = np.zeros((128, 1024), dtype=np.float32)
    for w in range(8):
        if w > c:
            m2[127, w * 128:(w + 1) * 128] = NEGBIG
        elif w == c:
            for s_ in range(1, 128):
                m2[s_ - 1, w * 128 + s_] = NEGBIG
    return m1, m2


def _inputs_for_core(c, x, shared, qnw, knw):
    rows = []
    for t in range(NSLOT):
        b, J = slot_bj(t, c)
        rows.append(x[b, J * 128:(J + 1) * 128, :])
    xt = np.ascontiguousarray(np.concatenate(rows, axis=0).T)
    m1, m2 = _masks_for_core(c)
    return {
        "XT": xt,
        "WQT": shared["WQT"], "WKT": shared["WKT"], "WVT": shared["WVT"],
        "WOT": shared["WOT"],
        "WG": np.ascontiguousarray(shared["WGT_full"][512 * c:512 * (c + 1), :]),
        "WU": np.ascontiguousarray(shared["WUT_full"][512 * c:512 * (c + 1), :]),
        "WD": np.ascontiguousarray(shared["WDT_full"][:, 512 * c:512 * (c + 1)]),
        "QNW": np.ascontiguousarray(qnw.reshape(128, 1)),
        "KNW": np.ascontiguousarray(knw.reshape(128, 1)),
        "MSK1": m1, "MSK2": m2,
    }


def make_in_maps(L0_attn_x, L0_attn_Wq, L0_attn_Wk, L0_attn_Wv, L0_attn_Wo,
                 L0_attn_q_norm_w, L0_attn_k_norm_w,
                 L0_ffn_w_gate, L0_ffn_w_up, L0_ffn_w_down):
    x = np.asarray(L0_attn_x, dtype=np.float32)
    shared = _prep_shared(np.asarray(L0_attn_Wq, np.float32),
                          np.asarray(L0_attn_Wk, np.float32),
                          np.asarray(L0_attn_Wv, np.float32),
                          np.asarray(L0_attn_Wo, np.float32),
                          np.asarray(L0_ffn_w_gate, np.float32),
                          np.asarray(L0_ffn_w_up, np.float32),
                          np.asarray(L0_ffn_w_down, np.float32))
    qnw = np.asarray(L0_attn_q_norm_w, np.float32)
    knw = np.asarray(L0_attn_k_norm_w, np.float32)
    return [_inputs_for_core(c, x, shared, qnw, knw) for c in range(NCORE)]


def run(in_maps, taps=()):
    nc = build(taps=taps)
    return run_bass_kernel_spmd(nc, in_maps, core_ids=list(range(NCORE)))


def kernel(**inputs):
    in_maps = make_in_maps(**inputs)
    res = run(in_maps)
    out = np.empty((BATCH, SEQ, DM), dtype=np.float32)
    for c in range(NCORE):
        oc = res.results[c]["OUT"]
        for t in range(NSLOT):
            b, J = slot_bj(t, c)
            out[b, J * 128:(J + 1) * 128, :] = oc[:, t * 128:(t + 1) * 128].T
    return out


# version 16 (final)
# speedup vs baseline: 1.1005x; 1.1005x over previous
import math
from contextlib import ExitStack

import numpy as np
import ml_dtypes

import concourse.bass as bass
import concourse.mybir as mybir
from concourse.bass_utils import run_bass_kernel_spmd

F32 = mybir.dt.float32
BF16 = mybir.dt.bfloat16
AF = mybir.ActivationFunctionType
ALU = mybir.AluOpType
AX = mybir.AxisListType
BF = ml_dtypes.bfloat16

DM = 4096
SEQ = 4096
BATCH = 2
NH = 32
NKV = 8
HD = 128
IFD = 11008
NCORE = 8
TOK = 1024
NSLOT = 8
NCI = IFD // 128
NDMC = DM // 128
MAGIC = 12582912.0
NEGBIG = -1e30
NWEL = float(IFD) * float(DM)
EPS_RMS = 1e-6
EPS_LN = 1e-5
EPS_BIT = 1e-5
RSQ_HD = 1.0 / math.sqrt(HD)

QT_GU = [640] * 17 + [128]
DEBUG_P2_UNITS = None


def _slopes64():
    start = 2 ** (-(2 ** (-(math.log2(NH) - 3))))
    return np.array([start * start ** i for i in range(NH)], dtype=np.float64)


def slot_bj(t, c):
    return t & 1, 8 * (t >> 1) + c


class Sem:
    __slots__ = ("name", "h", "n")

    def __init__(self, name):
        self.name = name
        self.h = None
        self.n = 0

    def inc(self, k=1):
        self.n += k
        return self.n


class Res:
    __slots__ = ("g",)

    def __init__(self):
        self.g = []

    def read_by(self, sem, n):
        self.g.append((sem, n))


class KB:
    ENGINES = ("sync", "gpsimd", "tensor", "scalar", "vector")

    def __init__(self):
        self.nc = bass.Bass("TRN2", target_bir_lowering=False, num_devices=NCORE)
        self.streams = {e: [] for e in self.ENGINES}
        self.sems = {}

    def sem(self, name):
        s = self.sems.get(name)
        if s is None:
            s = Sem(name)
            self.sems[name] = s
        return s

    def em(self, eng, fn):
        self.streams[eng].append(fn)

    def wait(self, eng, sem, val):
        if val > 0:
            self.em(eng, lambda E, s=sem, v=val: E.wait_ge(s.h, v))

    def guard(self, eng, res):
        for (s, n) in res.g:
            self.wait(eng, s, n)
        res.g = []

    def dma(self, q, out_ap, in_ap, sem):
        n = sem.inc(16)
        self.em(q, lambda E, o=out_ap, i=in_ap, s=sem:
                E.dma_start(o, i).then_inc(s.h, 16))
        return n

    def mm(self, out, lhsT, rhs, start, stop, sig=False):
        sem = self.sem("pe") if (sig or stop) else None
        n = sem.inc() if sem else None

        def f(E, o=out, l=lhsT, r=rhs, st=start, sp=stop, s=sem):
            ins = E.matmul(o, l, r, start=st, stop=sp)
            if s is not None:
                ins.then_inc(s.h, 1)
        self.em("tensor", f)
        return n

    def tr(self, out, in_, ident):
        sem = self.sem("pe")
        n = sem.inc()

        def f(E, o=out, i=in_, d=ident, s=sem):
            E.transpose(o, i, d).then_inc(s.h, 1)
        self.em("tensor", f)
        return n

    def act(self, out, in_, func, bias=0.0, scale=1.0, accum_out=None):
        sem = self.sem("act")
        n = sem.inc()

        def f(E, o=out, i=in_, fu=func, b=bias, sc=scale, ao=accum_out, s=sem):
            E.activation(o, i, fu, bias=b, scale=sc, accum_out=ao).then_inc(s.h, 1)
        self.em("scalar", f)
        return n

    def vop(self, fn):
        self.em("vector", fn)

    def vsig(self):
        s = self.sem("dve")
        n = s.inc()
        self.em("vector", lambda E, ss=s: E.drain().then_inc(ss.h, 1))
        return n

    def quiesce(self, queues=("sync", "gpsimd")):
        for qn in queues:
            for sname in ("pe", "act", "dve", "pool"):
                if sname in self.sems:
                    self.wait(qn, self.sems[sname], self.sems[sname].n)

    def emit(self, block):
        for name in self.ENGINES:
            fns = self.streams[name]

            def runner(E, fns=fns):
                for fn in fns:
                    fn(E)
            getattr(block, name)(runner)


def build(taps=()):
    taps = set(taps)
    k = KB()
    nc = k.nc
    S = k.sem

    XT = nc.dram_tensor("XT", [DM, TOK], F32, kind="ExternalInput").ap()
    WQT = nc.dram_tensor("WQT", [NH, DM, HD], F32, kind="ExternalInput").ap()
    WKT = nc.dram_tensor("WKT", [NKV, DM, HD], F32, kind="ExternalInput").ap()
    WVT = nc.dram_tensor("WVT", [2, DM, 512], F32, kind="ExternalInput").ap()
    WOT = nc.dram_tensor("WOT", [NDMC, DM, 128], F32, kind="ExternalInput").ap()
    WG = nc.dram_tensor("WG", [512, IFD], F32, kind="ExternalInput").ap()
    WU = nc.dram_tensor("WU", [512, IFD], F32, kind="ExternalInput").ap()
    WD = nc.dram_tensor("WD", [IFD, 512], F32, kind="ExternalInput").ap()
    QNW = nc.dram_tensor("QNW", [128, 1], F32, kind="ExternalInput").ap()
    KNW = nc.dram_tensor("KNW", [128, 1], F32, kind="ExternalInput").ap()
    MSK1 = nc.dram_tensor("MSK1", [128, 1024], F32, kind="ExternalInput").ap()
    MSK2 = nc.dram_tensor("MSK2", [128, 1024], F32, kind="ExternalInput").ap()
    OUT = nc.dram_tensor("OUT", [DM, TOK], F32, kind="ExternalOutput").ap()

    POSD = nc.inline_tensor(np.arange(SEQ, dtype=np.float32).reshape(1, SEQ),
                            name="posrow_d").ap()
    pos2_np = np.ones((2, SEQ), dtype=np.float32)
    pos2_np[1, :] = np.arange(SEQ, dtype=np.float32)
    POS2D = nc.inline_tensor(pos2_np, name="pos2_d").ap()
    ONERD = nc.inline_tensor(np.ones((1, 128), dtype=np.float32),
                             name="onerow_d").ap()
    slopes64 = _slopes64()
    slopea_np = np.zeros((1, SEQ), dtype=np.float32)
    for h in range(NH):
        slopea_np[0, h * 128:(h + 1) * 128] = slopes64[h]
    SLOPEAD = nc.inline_tensor(slopea_np, name="slopea_d").ap()
    posi = np.arange(SEQ)
    poshl_np = np.zeros((2, SEQ), dtype=BF)
    poshl_np[0, :] = (posi >> 5).astype(np.float32)
    poshl_np[1, :] = (posi & 31).astype(np.float32)
    POSHLD = nc.inline_tensor(poshl_np, name="poshl_d").ap()
    pos4_np = np.ones((4, SEQ), dtype=BF)
    pos4_np[2, :] = poshl_np[0, :]
    pos4_np[3, :] = poshl_np[1, :]
    POS4D = nc.inline_tensor(pos4_np, name="pos4_d").ap()
    slb16 = slopes64.astype(BF)
    sl2_np = np.zeros((2, SEQ), dtype=BF)
    for h in range(NH):
        sl2_np[0, h * 128:(h + 1) * 128] = BF(np.float32(slb16[h]) * 32.0)
        sl2_np[1, h * 128:(h + 1) * 128] = slb16[h]
    SL2D = nc.inline_tensor(sl2_np, name="sl2_d").ap()
    CONSTD = nc.inline_tensor(
        np.repeat(np.array([[EPS_RMS, EPS_LN]], dtype=np.float32), 128, axis=0),
        name="consts_d").ap()
    GE01D = nc.inline_tensor(
        (np.arange(128)[:, None] >= np.arange(128)[None, :]).astype(BF), name="ge01").ap()
    IDFD = nc.inline_tensor(np.eye(128, dtype=np.float32), name="identf").ap()
    IDBD = nc.inline_tensor(np.eye(128, dtype=BF), name="identb").ap()
    ONESBD = nc.inline_tensor(np.ones((128, 1), dtype=BF), name="onesb").ap()
    ONESFD = nc.inline_tensor(np.ones((128, 1), dtype=np.float32), name="onesf").ap()

    QSC = nc.dram_tensor("qsc", [NH * 128, TOK], BF16, kind="Internal").ap()
    K_IN = nc.dram_tensor("k_in", [NKV * 128, TOK], BF16, kind="Internal").ap()
    K_OUT = nc.dram_tensor("k_out", [NCORE * NKV * 128, TOK], BF16, kind="Internal").ap()
    V_IN = nc.dram_tensor("v_in", [NKV * TOK, HD], BF16, kind="Internal").ap()
    V_OUT = nc.dram_tensor("v_out", [NCORE * NKV * TOK, HD], BF16, kind="Internal").ap()
    FGU_IN = nc.dram_tensor("fgu_in", [2 * 4 * IFD, 128], BF16, kind="Internal").ap()
    FGU_OUT = nc.dram_tensor("fgu_out", [NCORE * 2 * 4 * IFD, 128], BF16,
                             kind="Internal").ap()
    FD_IN = nc.dram_tensor("fd_in", [4 * IFD, 128], BF16, kind="Internal").ap()
    FD_OUT = nc.dram_tensor("fd_out", [NCORE * 4 * IFD, 128], BF16,
                            kind="Internal").ap()
    AR_IN = nc.dram_tensor("ar_in", [1, 8], F32, kind="Internal").ap()
    AR_OUT = nc.dram_tensor("ar_out", [1, 8], F32, kind="Internal").ap()
    X1SC = nc.dram_tensor("x1sc", [DM, TOK], F32, kind="Internal").ap()
    X1N = nc.dram_tensor("x1n", [DM, TOK], F32, kind="Internal").ap()
    X2SC = nc.dram_tensor("x2sc", [DM, TOK], F32, kind="Internal").ap()
    INTER = nc.dram_tensor("inter", [NCI * 128, TOK], BF16, kind="Internal").ap()

    RG = [list(range(NCORE))]
    FGU_RANK = 2 * 4 * IFD
    FD_RANK = 4 * IFD

    tap_list = []

    def tap(name, src_ap):
        if name in taps:
            t = nc.dram_tensor("TAP_" + name, list(src_ap.shape), F32,
                               kind="ExternalOutput").ap()
            tap_list.append((t, src_ap))

    with ExitStack() as top:
        sbt = lambda n, s, d: top.enter_context(nc.sbuf_tensor(n, s, d))

        ONEROW = sbt("onerow", [1, 128], F32)
        CONSTS = sbt("consts", [128, 2], F32)
        GE01 = sbt("ge01s", [128, 128], BF16)
        IDF = sbt("idfs", [128, 128], F32)
        IDB = sbt("idbs", [128, 128], BF16)
        ONESB = sbt("onesbs", [128, 1], BF16)
        ONESF = sbt("onesfs", [128, 1], F32)
        MSK1S = sbt("msk1s", [128, 1024], BF16)
        MSK2S = sbt("msk2s", [128, 1024], BF16)
        QNWS = sbt("qnws", [128, 1], F32)
        KNWS = sbt("knws", [128, 1], F32)
        QNWSC = sbt("qnwsc", [128, 1], F32)
        INVC = sbt("invc", [128, 4], F32)
        S3C = sbt("s3c", [128, 1], F32)
        SCROW = sbt("scrow", [1, 8], F32)
        SCSUM = sbt("scsum", [1, 8], F32)
        SCT = sbt("sct", [1, 8], F32)
        ABSM = sbt("absm", [128, 4], F32)
        ABST = sbt("abst", [128, 1], F32)
        SPC = sbt("spc", [128, 1], F32)
        DUMP = sbt("dump", [128, 1024], F32)
        NM = sbt("nm", [128, 16], F32)
        EBF = sbt("ebf", [128, 16], F32)
        NMA = sbt("nma", [128, 4], F32)
        TT4 = sbt("tt4", [128, 4], F32)
        SIGB = sbt("sigb", [128, 4], F32)
        LNSB = sbt("lnsb", [128, 4], F32)
        BCOL = sbt("bcol", [128, 8], F32)
        BROW4 = sbt("brow4", [1, 512], F32)

        MBB = sbt("mbb", [128, 1024], F32)
        IBB = sbt("ibb", [128, 1024], F32)
        gmbb = Res()
        ao_stack = ExitStack()
        AO = ao_stack.enter_context(
            nc.sbuf_tensor("ao", [128, NH * TOK], BF16))

        g = {name: Res() for name in (
            "psq0", "psq1", "psst", "psbb", "qraw", "sqb", "srow", "irow",
            "chb0", "chb1", "dump", "qa0", "qa1", "wb0", "wb1", "vw0", "vw1",
            "kst0", "kst1", "vst0", "vst1", "scrow", "absm",
        )}

        n_sm = 0
        nc.const_aps.register(F32, EPS_RMS, CONSTS[:, 0:1])
        nc.const_aps.register(F32, EPS_LN, CONSTS[:, 1:2])
        for dst, src in ((ONEROW[:, :], ONERD), (CONSTS[:, :], CONSTD),
                         (GE01[:, :], GE01D), (IDF[:, :], IDFD),
                         (IDB[:, :], IDBD),
                         (ONESB[:, :], ONESBD), (ONESF[:, :], ONESFD),
                         (QNWS[:, :], QNW), (KNWS[:, :], KNW)):
            n_sm = k.dma("sync", dst, src, S("dsm"))
        n_m1 = k.dma("sync", DUMP[:, :], MSK1, S("dsm"))
        k.wait("vector", S("dsm"), n_m1)
        k.vop(lambda E: E.tensor_scalar(QNWSC[:, :], QNWS[:, :], RSQ_HD, None, ALU.mult))
        k.vop(lambda E: E.tensor_copy(MSK1S[:, :], DUMP[:, :]))
        n_c1 = k.vsig()
        k.wait("sync", S("dve"), n_c1)
        n_sm = k.dma("sync", DUMP[:, :], MSK2, S("dsm"))
        k.wait("vector", S("dsm"), n_sm)
        k.vop(lambda E: E.tensor_copy(MSK2S[:, :], DUMP[:, :]))
        n_mskcast = k.vsig()

        with ExitStack() as p1:
            sb1 = lambda n, s, d: p1.enter_context(nc.sbuf_tensor(n, s, d))
            xb = sb1("xb", [128, NDMC * TOK], BF16)
            WB = [sb1("wb0", [128, NDMC * 128], BF16),
                  sb1("wb1", [128, NDMC * 128], BF16)]
            KST = [sb1("kst0", [128, 512], BF16), sb1("kst1", [128, 512], BF16)]
            VW = [sb1("vw0", [128, 512], BF16), sb1("vw1", [128, 512], BF16)]
            VST = [sb1("vst0", [128, 512], BF16), sb1("vst1", [128, 512], BF16)]
            QRAW = sb1("qraw", [128, 512], F32)
            SQB = sb1("sqb", [128, 512], BF16)
            QA = [sb1("qa0", [128, 640], F32), sb1("qa1", [128, 640], F32)]
            SROW = sb1("srow", [1, 512], F32)
            IROW = sb1("irow", [1, 512], F32)

            n_xb = 0
            for piece in range(4):
                src = XT[piece * 1024:(piece + 1) * 1024, :].rearrange(
                    "(a p) t -> p a t", p=128)
                dst = xb[:, piece * 8 * TOK:(piece + 1) * 8 * TOK].rearrange(
                    "p (a t) -> p a t", t=TOK)
                n_xb = k.dma("gpsimd", dst, src, S("dxb"))

            qa_tiles = []
            for m, wsrc in ((0, WG), (1, WU)):
                for rc in range(4):
                    c0 = 0
                    for w in QT_GU:
                        qa_tiles.append((m, wsrc, rc, c0, w))
                        c0 += w
            for rc in range(NCI):
                qa_tiles.append((2, WD, rc, 0, 512))
            qa_nload = {}

            def qa_load(i):
                m, wsrc, rc, c0, w = qa_tiles[i]
                buf = QA[i % 2]
                k.guard("gpsimd", g["qa%d" % (i % 2)])
                qa_nload[i] = k.dma("gpsimd", buf[:, 0:w],
                                    wsrc[rc * 128:(rc + 1) * 128, c0:c0 + w],
                                    S("dqa%d" % (i % 2)))

            def qa_reduce(i):
                m, wsrc, rc, c0, w = qa_tiles[i]
                buf = QA[i % 2]
                k.wait("vector", S("dqa%d" % (i % 2)), qa_nload[i])
                first = (i == 0) or (m != qa_tiles[i - 1][0])
                if first:
                    k.vop(lambda E, b=buf, ww=w, mm_=m: E.reduce_sum(
                        ABSM[:, mm_:mm_ + 1], b[:, 0:ww], AX.X,
                        apply_absolute_value=True))
                else:
                    k.vop(lambda E, b=buf, ww=w: E.reduce_sum(
                        ABST[:, :], b[:, 0:ww], AX.X, apply_absolute_value=True))
                    k.vop(lambda E: E.tensor_copy(SPC[:, :], SPC[:, :]))
                    k.vop(lambda E, mm_=m: E.tensor_tensor(
                        ABSM[:, mm_:mm_ + 1], ABSM[:, mm_:mm_ + 1], ABST[:, :],
                        ALU.add))
                nv = k.vsig()
                g["qa%d" % (i % 2)].read_by(S("dve"), nv)

            qa_load(0)
            qa_load(1)
            qa_state = [2, 0]

            def qa_step():
                if qa_state[1] < len(qa_tiles):
                    qa_reduce(qa_state[1])
                    qa_state[1] += 1
                if qa_state[0] < len(qa_tiles):
                    qa_load(qa_state[0])
                    qa_state[0] += 1

            def wblock_load(bi, src3, fc, sem):
                k.guard("gpsimd", g["wb%d" % bi])
                w = src3.shape[2]
                src = src3[fc].rearrange("(a p) f -> p a f", p=128)
                dst = WB[bi][:, 0:NDMC * w].rearrange("p (a f) -> p a f", f=w)
                return k.dma("gpsimd", dst, src, sem)

            with ExitStack() as ppv:
                psV = [ppv.enter_context(nc.psum_tensor("psv%d" % i, [128, 512], F32))
                       for i in range(8)]
                gv = [Res() for _ in range(8)]
                k.wait("tensor", S("dxb"), n_xb)
                k.wait("tensor", S("dsm"), n_sm)
                k.wait("tensor", S("act"), k.sems["act"].n)
                k.wait("tensor", S("dve"), k.sems["dve"].n)

                def vw_load(ti):
                    bi = ti % 2
                    k.guard("gpsimd", g["vw%d" % bi])
                    return k.dma("gpsimd", VW[bi][:, :],
                                 WVT[ti // NDMC,
                                     (ti % NDMC) * 128:(ti % NDMC) * 128 + 128, :],
                                 S("dwv%d" % bi))

                nvw = [vw_load(0), vw_load(1)]
                n_vst = 0
                for fh in range(2):
                    for dmc in range(NDMC):
                        ti = fh * NDMC + dmc
                        bi = ti % 2
                        k.wait("tensor", S("dwv%d" % bi), nvw[bi])
                        if dmc == 0:
                            for p in range(8):
                                k.guard("tensor", gv[p])
                        for p in range(8):
                            k.mm(psV[p][:, :],
                                 xb[:, dmc * TOK + p * 128: dmc * TOK + p * 128 + 128],
                                 VW[bi][:, :],
                                 start=(dmc == 0), stop=(dmc == NDMC - 1),
                                 sig=(p == 7))
                        npe = k.sems["pe"].n
                        g["vw%d" % bi].read_by(S("pe"), npe)
                        if ti + 2 < 2 * NDMC:
                            nvw[bi] = vw_load(ti + 2)
                    npe = k.sems["pe"].n
                    for p in range(8):
                        vi = p % 2
                        k.wait("scalar", S("pe"), npe)
                        k.guard("scalar", g["vst%d" % vi])
                        na = k.act(VST[vi][:, :], psV[p][:, :], AF.Identity)
                        gv[p].read_by(S("act"), na)
                        k.wait("sync", S("act"), na)
                        for kk in range(4):
                            row0 = (fh * 4 + kk) * TOK + p * 128
                            n_vst = k.dma("sync", V_IN[row0:row0 + 128, :],
                                          VST[vi][:, kk * 128:(kk + 1) * 128],
                                          S("dkvst"))
                        g["vst%d" % vi].read_by(S("dkvst"), n_vst)
                    qa_step()
            nwk = [wblock_load(0, WKT, 0, S("dwkv0")),
                   wblock_load(1, WKT, 1, S("dwkv1"))]
            k.wait("gpsimd", S("dkvst"), k.sems["dkvst"].n)
            k.em("gpsimd", lambda E: E.collective_compute(
                "AllGather", ALU.bypass, replica_groups=RG,
                ins=[V_IN], outs=[V_OUT]).then_inc(k.sems["cc"].h, 1))
            S("cc").inc()

            with ExitStack() as pp:
                ps = lambda n, s: pp.enter_context(nc.psum_tensor(n, s, F32))
                psQ = [ps("psq0", [128, 512]), ps("psq1", [128, 512])]
                psSt = ps("psst", [1, 512])
                psB = ps("psbb", [128, 512])
                k.wait("tensor", S("act"), k.sems["act"].n)
                k.wait("tensor", S("dve"), k.sems["dve"].n)

                def proj_norm(idx, WBu, gwb, wsem, nw, nrm_w, dst_dram,
                              dst_row, dsem):
                    for half in range(2):
                        par = (idx * 2 + half) % 2
                        pq = psQ[par]
                        gq = g["psq%d" % par]
                        k.wait("tensor", wsem, nw)
                        k.guard("tensor", gq)
                        for dmc in range(NDMC):
                            k.mm(pq[:, :], WBu[:, dmc * 128:(dmc + 1) * 128],
                                 xb[:, dmc * TOK + half * 512: dmc * TOK + half * 512 + 512],
                                 start=(dmc == 0), stop=(dmc == NDMC - 1))
                        npe = k.sems["pe"].n
                        gwb.read_by(S("pe"), npe)
                        k.wait("scalar", S("pe"), npe)
                        k.guard("scalar", g["sqb"])
                        k.act(SQB[:, :], pq[:, :], AF.Square)
                        k.guard("scalar", g["qraw"])
                        na1 = k.act(QRAW[:, :], pq[:, :], AF.Identity)
                        gq.read_by(S("act"), na1)
                        k.wait("tensor", S("act"), na1)
                        k.guard("tensor", g["psst"])
                        k.mm(psSt[:, :], ONESB[:, :], SQB[:, :], start=True, stop=True)
                        npe2 = k.sems["pe"].n
                        g["sqb"].read_by(S("pe"), npe2)
                        k.wait("scalar", S("pe"), npe2)
                        k.guard("scalar", g["srow"])
                        na2 = k.act(SROW[:, :], psSt[:, :], AF.Sqrt,
                                    scale=1.0 / HD, bias=EPS_RMS)
                        g["psst"].read_by(S("act"), na2)
                        k.wait("vector", S("act"), na2)
                        k.guard("vector", g["irow"])
                        k.vop(lambda E: E.reciprocal(IROW[:, :], SROW[:, :]))
                        nv = k.vsig()
                        g["srow"].read_by(S("dve"), nv)
                        k.wait("tensor", S("dve"), nv)
                        k.guard("tensor", g["psbb"])
                        k.mm(psB[:, :], ONEROW[:, :], IROW[:, :],
                             start=True, stop=True)
                        npe3 = k.sems["pe"].n
                        g["irow"].read_by(S("pe"), npe3)
                        k.wait("vector", S("pe"), npe3)
                        k.vop(lambda E: E.tensor_tensor(QRAW[:, :], QRAW[:, :],
                                                        psB[:, :], ALU.mult))
                        ksti = (idx * 2 + half) % 2
                        k.guard("vector", g["kst%d" % ksti])
                        k.vop(lambda E, d=KST[ksti]: E.tensor_scalar(
                            d[:, :], QRAW[:, :], nrm_w[:, :], None, ALU.mult))
                        nv2 = k.vsig()
                        k.wait("sync", S("dve"), nv2)
                        nst = k.dma("sync",
                                    dst_dram[dst_row * 128:(dst_row + 1) * 128,
                                             half * 512:half * 512 + 512],
                                    KST[ksti][:, :], dsem)
                        g["kst%d" % ksti].read_by(dsem, nst)
                        g["psbb"].read_by(S("dve"), nv2)
                        g["qraw"].read_by(S("dve"), nv2)
                    return

                for fc in range(NKV):
                    bi = fc % 2
                    proj_norm(fc, WB[bi], g["wb%d" % bi], S("dwkv%d" % bi),
                              nwk[bi], KNWS, K_IN, fc, S("dkvst"))
                    if fc + 2 < NKV:
                        nwk[bi] = wblock_load(bi, WKT, fc + 2, S("dwkv%d" % bi))
                    qa_step()
                nwq = [wblock_load(0, WQT, 0, S("dwq0")),
                       wblock_load(1, WQT, 1, S("dwq1"))]
                k.wait("gpsimd", S("dkvst"), k.sems["dkvst"].n)
                k.em("gpsimd", lambda E: E.collective_compute(
                    "AllGather", ALU.bypass, replica_groups=RG,
                    ins=[K_IN], outs=[K_OUT]).then_inc(k.sems["cc"].h, 1))
                n_cc_kv = S("cc").inc()

                k.wait("tensor", S("dxb"), n_xb)
                k.wait("tensor", S("dsm"), n_sm)
                for fc in range(NH):
                    bi = fc % 2
                    proj_norm(fc, WB[bi], g["wb%d" % bi], S("dwq%d" % bi),
                              nwq[bi], QNWSC, QSC, fc, S("dqst"))
                    if fc + 2 < NH:
                        nwq[bi] = wblock_load(bi, WQT, fc + 2, S("dwq%d" % bi))
                    qa_step()

            while qa_state[1] < len(qa_tiles):
                qa_step()

        with ExitStack() as pq2:
            psS = pq2.enter_context(nc.psum_tensor("psscal", [1, 8], F32))
            psSB = pq2.enter_context(nc.psum_tensor("psscalb", [128, 8], F32))
            nv = k.vsig()
            k.wait("tensor", S("dve"), nv)
            k.mm(psS[:, 0:3], ONESF[:, :], ABSM[:, 0:3], start=True, stop=True)
            npe = k.sems["pe"].n
            k.wait("vector", S("pe"), npe)
            k.vop(lambda E: E.memset(SCROW[:, :], 0.0))
            k.vop(lambda E: E.tensor_copy(SCROW[:, 0:3], psS[:, 0:3]))
            nv2 = k.vsig()
            k.wait("sync", S("dve"), nv2)
            nar = k.dma("sync", AR_IN, SCROW[:, :], S("dar"))
            k.wait("gpsimd", S("dar"), nar)
            k.em("gpsimd", lambda E: E.collective_compute(
                "AllReduce", ALU.add, replica_groups=RG,
                ins=[AR_IN], outs=[AR_OUT]).then_inc(k.sems["cc"].h, 1))
            n_cc_ar = S("cc").inc()
            k.wait("sync", S("cc"), n_cc_ar)
            nar2 = k.dma("sync", SCSUM[:, :], AR_OUT, S("dar"))
            k.wait("vector", S("dar"), nar2)
            spc = lambda: k.vop(lambda E: E.tensor_copy(SPC[:, :], SPC[:, :]))
            k.vop(lambda E: E.tensor_scalar(SCT[:, :], SCSUM[:, :], 1.0 / NWEL,
                                            None, ALU.mult))
            spc()
            k.vop(lambda E: E.tensor_scalar(SCT[0:1, 5:8], SCT[0:1, 0:3], EPS_BIT,
                                            None, ALU.add))
            spc()
            k.vop(lambda E: E.reciprocal(SCROW[0:1, 0:3], SCT[0:1, 5:8]))
            k.vop(lambda E: E.tensor_tensor(SCT[0:1, 4:5], SCT[0:1, 0:1],
                                            SCT[0:1, 1:2], ALU.mult))
            spc()
            k.vop(lambda E: E.tensor_tensor(SCT[0:1, 4:5], SCT[0:1, 4:5],
                                            SCT[0:1, 2:3], ALU.mult))
            nv3 = k.vsig()
            k.wait("tensor", S("dve"), nv3)
            k.mm(psSB[:, 0:4], ONEROW[:, :], SCROW[0:1, 0:4], start=True,
                 stop=True)
            npe2 = k.sems["pe"].n
            k.wait("vector", S("pe"), npe2)
            k.vop(lambda E: E.tensor_copy(INVC[:, :], psSB[:, 0:4]))
            nv4 = k.vsig()
            k.wait("tensor", S("dve"), nv4)
            k.mm(psSB[:, 0:1], ONEROW[:, :], SCT[0:1, 4:5], start=True,
                 stop=True)
            npe3 = k.sems["pe"].n
            k.wait("vector", S("pe"), npe3)
            k.vop(lambda E: E.tensor_copy(S3C[:, :], psSB[:, 0:1]))
            k.vsig()

        tap("Q", QSC)
        tap("VIN", V_IN)
        tap("ABSM", ABSM[:, 0:3])
        tap("SCSUM", SCSUM[:, :])
        tap("KAG", K_OUT)
        tap("VAG", V_OUT)
        tap("SCT", SCT[:, :])

        with ExitStack() as p2:
            sb2 = lambda n, s, d: p2.enter_context(nc.sbuf_tensor(n, s, d))
            KBUF = [sb2("kbuf%d" % i, [128, SEQ], BF16) for i in range(4)]
            CHB = [sb2("chb%d" % i, [128, 1024], F32) for i in range(4)]
            VBUF = [sb2("vbuf%d" % i, [128, SEQ], BF16) for i in range(2)]
            QT = [sb2("qt%d" % i, [128, 512], BF16) for i in range(4)]
            PRT = [sb2("prt0", [128, 512], BF16), sb2("prt1", [128, 512], BF16)]
            EXPD = sb2("expd", [128, 4], F32)
            POSHL = sb2("poshl", [2, SEQ], BF16)
            POS4 = sb2("pos4", [4, SEQ], BF16)
            SL2 = sb2("sl2", [2, SEQ], BF16)
            SLB4 = sb2("slb4", [4, SEQ], BF16)
            k.quiesce()
            n_p2sm = k.dma("sync", POSHL[:, :], POSHLD, S("dsm"))
            n_p2sm = k.dma("sync", POS4[:, :], POS4D, S("dsm"))
            n_p2sm = k.dma("sync", SL2[:, :], SL2D, S("dsm"))
            n_p2sm = k.dma("sync", SLB4[2:4, :], SL2D, S("dsm"))
            k.wait("tensor", S("dsm"), n_p2sm)
            QBL = [sb2("qbl0", [128, 640], F32), sb2("qbl1", [128, 640], F32)]
            R1B = [sb2("r1b0", [128, 640], F32), sb2("r1b1", [128, 640], F32)]
            QBO = [sb2("qbo0", [128, 640], BF16), sb2("qbo1", [128, 640], BF16)]
            ps1 = [p2.enter_context(nc.psum_tensor("ps1_%d" % i, [128, 1024], F32))
                   for i in range(2)]
            ps2 = [p2.enter_context(nc.psum_tensor("ps2_%d" % i, [128, 512], F32))
                   for i in range(2)]
            psAV = p2.enter_context(nc.psum_tensor("psav", [128, 128], F32))
            psBT = p2.enter_context(nc.psum_tensor("psbt", [2, 512], BF16))
            BCB = sb2("bcb", [128, 16], BF16)

            gk = [Res() for _ in range(4)]
            gvv = [Res(), Res()]
            gqt = [Res() for _ in range(4)]
            gprt = [Res(), Res()]
            gps1 = [Res(), Res()]
            gps2 = [Res(), Res()]
            gav = Res()
            gbt = Res()
            gbrow = Res()
            gbcb_a = [Res(), Res()]
            gbcb_v = [Res(), Res()]
            gbcol = [Res(), Res()]
            gchb = [Res(), Res(), Res(), Res()]
            gnm = Res()
            gstat = Res()

            units = [(t, kv) for t in range(NSLOT) for kv in range(NKV)]
            if DEBUG_P2_UNITS is not None:
                units = [units[i] for i in DEBUG_P2_UNITS]

            K6 = K_OUT.rearrange("(cc kv d) (tp s) -> kv cc d tp s",
                                 kv=NKV, d=128, s=128)
            V6 = V_OUT.rearrange("(cc kv tp s) d -> kv cc tp s d",
                                 kv=NKV, tp=8, s=128)
            Q6 = QSC.rearrange("(hh p) tok -> p hh tok", p=128)

            nK = [0, 0, 0, 0]
            nQ = [0, 0, 0, 0]
            nV = [0, 0]

            def emit_kq_loads(u):
                t, kv = units[u]
                gg1 = (t >> 1) + 1
                b = t & 1
                par = u % 4
                k.guard("sync", gk[par])
                for gg in range(gg1):
                    tp = 2 * gg + b
                    src = K6[kv, :, :, tp, :].transpose([1, 0, 2])
                    dst = KBUF[par][:, gg * 1024:(gg + 1) * 1024].rearrange(
                        "p (cc s) -> p cc s", cc=8)
                    nK[par] = k.dma("sync", dst, src, S("dkg%d" % par))
                k.guard("sync", gqt[par])
                qsrc = Q6[:, 4 * kv:4 * kv + 4, t * 128:(t + 1) * 128]
                qdst = QT[par][:, :].rearrange("p (hh s) -> p hh s", s=128)
                nQ[par] = k.dma("sync", qdst, qsrc, S("dqt%d" % par))

            def emit_v_load(u):
                t, kv = units[u]
                gg1 = (t >> 1) + 1
                b = t & 1
                vpar = u % 2
                k.guard("sync", gvv[vpar])
                for gg in range(gg1):
                    tp = 2 * gg + b
                    vsrc = V6[kv, :, tp, :, :].transpose([1, 0, 2])
                    vdst = VBUF[vpar][:, gg * 1024:(gg + 1) * 1024].rearrange(
                        "p (cc d) -> p cc d", cc=8)
                    nV[vpar] = k.dma("sync", vdst, vsrc, S("dvg"))

            qb_tiles = []
            for m, wsrc in ((0, WG), (1, WU)):
                for rc in range(4):
                    c0 = 0
                    for w in QT_GU:
                        qb_tiles.append((m, wsrc, rc, c0, w))
                        c0 += w
            for rc in range(NCI):
                qb_tiles.append((2, WD, rc, 0, 512))
            gqbl = [Res(), Res()]
            gqbo = [Res(), Res()]
            qb_nload = {}
            qb_state = [0, 0]

            def qb_load(i):
                m, wsrc, rc, c0, w = qb_tiles[i]
                k.guard("gpsimd", gqbl[i % 2])
                qb_nload[i] = k.dma("gpsimd", QBL[i % 2][:, 0:w],
                                    wsrc[rc * 128:(rc + 1) * 128, c0:c0 + w],
                                    S("dqb%d" % (i % 2)))

            def qb_proc(i):
                m, wsrc, rc, c0, w = qb_tiles[i]
                pi = i % 2
                k.wait("gpsimd", S("dqb%d" % pi), qb_nload[i])
                k.em("gpsimd", lambda E, pi=pi, w=w, m=m: E.tensor_scalar(
                    R1B[pi][:, 0:w], QBL[pi][:, 0:w], INVC[:, m:m + 1], MAGIC,
                    ALU.mult, ALU.add))
                k.em("gpsimd", lambda E, pi=pi, w=w: E.tensor_scalar(
                    R1B[pi][:, 0:w], R1B[pi][:, 0:w], -MAGIC, 1.0,
                    ALU.add, ALU.min))
                k.guard("gpsimd", gqbo[pi])
                k.em("gpsimd", lambda E, pi=pi, w=w: E.tensor_scalar(
                    QBO[pi][:, 0:w], R1B[pi][:, 0:w], -1.0, None, ALU.max))
                if m < 2:
                    base = m * 4 * IFD + rc * IFD + c0
                    nci_t = w // 128
                    dst = FGU_IN[base:base + w, :].rearrange(
                        "(ci p) f -> p ci f", p=128)
                    src = QBO[pi][:, 0:w].rearrange("p (ci f) -> p ci f", f=128)
                    nw_ = k.dma("gpsimd", dst, src, S("dffnst"))
                else:
                    for dd in range(4):
                        dst = FD_IN[dd * IFD + rc * 128: dd * IFD + rc * 128 + 128, :]
                        nw_ = k.dma("gpsimd", dst,
                                    QBO[pi][:, dd * 128:dd * 128 + 128],
                                    S("dffnst"))
                gqbo[pi].read_by(S("dffnst"), nw_)

            def qb_step(target):
                while qb_state[1] < min(target, len(qb_tiles)):
                    if qb_state[0] < min(qb_state[1] + 2, len(qb_tiles)):
                        qb_load(qb_state[0])
                        qb_state[0] += 1
                    if qb_state[0] >= qb_state[1] + 1:
                        qb_proc(qb_state[1])
                        qb_state[1] += 1

            qb_load(0)
            qb_load(1)
            qb_state[0] = 2
            qb_ag_done = [False]

            k.wait("sync", S("cc"), n_cc_kv)
            k.wait("sync", S("dqst"), k.sems["dqst"].n)
            emit_kq_loads(0)
            if len(units) > 1:
                emit_kq_loads(1)

            ustate = {}

            def emit_pass1(u):
                t, kv = units[u]
                gg1 = (t >> 1) + 1
                par = u % 4
                vpar = u % 2
                boff = (u % 2) * 4
                bo2 = (u % 2) * 8
                nKu, nQu = nK[par], nQ[par]
                emit_v_load(u)
                nVu = nV[vpar]
                if u + 2 < len(units):
                    emit_kq_loads(u + 2)

                k.wait("tensor", S("dkg%d" % par), nKu)
                k.wait("tensor", S("dqt%d" % par), nQu)

                for hh in range(4):
                    h = 4 * kv + hh
                    c0 = hh * 4
                    for n in range(gg1):
                        c1 = n % 2
                        u0 = n * 1024
                        k.guard("tensor", gps1[c1])
                        for q2 in range(2):
                            psl = ps1[c1][:, q2 * 512:q2 * 512 + 512]
                            cb = u0 + q2 * 512
                            k.mm(psl, QT[par][:, hh * 128:hh * 128 + 128],
                                 KBUF[par][:, cb:cb + 512],
                                 start=True, stop=False)
                            last = (n != gg1 - 1)
                            k.mm(psl, SL2[0:2, h * 128:h * 128 + 128],
                                 POSHL[0:2, cb:cb + 512],
                                 start=False, stop=last)
                            if not last:
                                k.mm(psl, GE01[:, :],
                                     MSK1S[:, q2 * 512:q2 * 512 + 512],
                                     start=False, stop=True)
                        npe = k.sems["pe"].n
                        k.wait("scalar", S("pe"), npe)
                        k.guard("scalar", gchb[n])
                        nev = k.act(CHB[n][:, :], ps1[c1][:, :], AF.Identity)
                        gps1[c1].read_by(S("act"), nev)
                    k.wait("vector", S("act"), k.sems["act"].n)
                    k.guard("vector", gnm)
                    for n in range(gg1):
                        k.vop(lambda E, n=n, col=c0 + n: E.tensor_reduce(
                            NM[:, col:col + 1], CHB[n][:, :], AX.X, ALU.max,
                            negate=True))
                    k.vop(lambda E: E.tensor_copy(SPC[:, :], SPC[:, :]))
                    k.vop(lambda E, c0=c0, C=gg1, hh=hh: E.tensor_reduce(
                        NMA[:, hh:hh + 1], NM[:, c0:c0 + C], AX.X, ALU.min))
                    nvm = k.vsig()
                    k.wait("scalar", S("dve"), nvm)
                    if hh == 0:
                        k.guard("scalar", gbcol[u % 2])
                        k.guard("scalar", gbcb_a[u % 2])
                    for n in range(gg1):
                        nex = k.act(DUMP[:, :], CHB[n][:, :], AF.Exp,
                                    bias=NMA[:, hh:hh + 1],
                                    accum_out=EBF[:, c0 + n:c0 + n + 1])
                        gchb[n].read_by(S("act"), nex)
                    k.act(DUMP[:, 0:gg1], EBF[:, c0:c0 + gg1], AF.Identity,
                          accum_out=SIGB[:, hh:hh + 1])
                    k.act(LNSB[:, hh:hh + 1], SIGB[:, hh:hh + 1], AF.Ln)
                    k.act(BCOL[:, boff + hh:boff + hh + 1], LNSB[:, hh:hh + 1],
                          AF.Identity, scale=-1.0, bias=NMA[:, hh:hh + 1])
                    k.act(BCB[:, bo2 + 2 * hh:bo2 + 2 * hh + 1],
                          LNSB[:, hh:hh + 1], AF.Identity, scale=-1.0,
                          bias=NMA[:, hh:hh + 1])
                gnm.read_by(S("dve"), nvm)
                na_tail = k.sems["act"].n
                ustate[u] = (par, vpar, boff, bo2, nVu, na_tail)

            def emit_pass2(u):
                t, kv = units[u]
                gg1 = (t >> 1) + 1
                nblk = gg1 * 8
                par, vpar, boff, bo2, nVu, na_tail = ustate.pop(u)

                k.wait("vector", S("act"), na_tail)
                k.guard("vector", gbcb_v[u % 2])
                for hh in range(4):
                    c = boff + hh
                    k.vop(lambda E, hh=hh, c=c, bo2=bo2: E.tensor_tensor(
                        BCB[:, bo2 + 2 * hh + 1:bo2 + 2 * hh + 2],
                        BCOL[:, c:c + 1],
                        BCB[:, bo2 + 2 * hh:bo2 + 2 * hh + 1], ALU.subtract))
                    k.vop(lambda E: E.tensor_copy(SPC[:, :], SPC[:, :]))
                nvbc = k.vsig()
                gbcol[u % 2].read_by(S("dve"), nvbc)
                k.wait("tensor", S("dve"), nvbc)
                k.guard("tensor", gbt)
                npbt = 0
                for hh in range(4):
                    npbt = k.tr(psBT[0:2, hh * 128:hh * 128 + 128],
                                BCB[:, bo2 + 2 * hh:bo2 + 2 * hh + 2], IDB[:, :])
                gbcb_a[u % 2].read_by(S("pe"), npbt)
                gbcb_v[u % 2].read_by(S("pe"), npbt)
                k.wait("vector", S("pe"), npbt)
                k.guard("vector", gbrow)
                for hh in range(4):
                    h = 4 * kv + hh
                    k.vop(lambda E, hh=hh, h=h: E.tensor_copy(
                        SLB4[0:2, h * 128:h * 128 + 128],
                        psBT[0:2, hh * 128:hh * 128 + 128]))
                nvb2 = k.vsig()
                gbt.read_by(S("dve"), nvb2)

                k.wait("tensor", S("dve"), nvb2)
                k.wait("tensor", S("dvg"), nVu)
                for hh in range(4):
                    h = 4 * kv + hh
                    for grp in range(nblk // 4):
                        p2i = grp % 2
                        k.guard("tensor", gps2[p2i])
                        k.guard("tensor", gprt[p2i])
                        for jj in range(4):
                            j = grp * 4 + jj
                            psl = ps2[p2i][:, jj * 128:jj * 128 + 128]
                            k.mm(psl, KBUF[par][:, j * 128:j * 128 + 128],
                                 QT[par][:, hh * 128:hh * 128 + 128],
                                 start=True, stop=False)
                            masked = (j >= nblk - 8)
                            k.mm(psl, POS4[0:4, j * 128:j * 128 + 128],
                                 SLB4[0:4, h * 128:h * 128 + 128],
                                 start=False, stop=(not masked))
                            if masked:
                                w = j - (nblk - 8)
                                k.mm(psl, MSK2S[:, w * 128:w * 128 + 128],
                                     GE01[:, :], start=False, stop=True)
                        npe2 = k.sems["pe"].n
                        k.wait("scalar", S("pe"), npe2)
                        nexp = k.act(PRT[p2i][:, :], ps2[p2i][:, :], AF.Exp)
                        gps2[p2i].read_by(S("act"), nexp)
                        k.wait("tensor", S("act"), nexp)
                        if grp == 0:
                            k.guard("tensor", gav)
                        for jj in range(4):
                            j = grp * 4 + jj
                            k.mm(psAV[:, :], VBUF[vpar][:, j * 128:j * 128 + 128],
                                 PRT[p2i][:, jj * 128:jj * 128 + 128],
                                 start=(j == 0), stop=(j == nblk - 1),
                                 sig=(jj == 3))
                        gprt[p2i].read_by(S("pe"), k.sems["pe"].n)
                    npe3 = k.sems["pe"].n
                    k.wait("scalar", S("pe"), npe3)
                    nctx = k.act(AO[:, h * TOK + t * 128: h * TOK + t * 128 + 128],
                                 psAV[:, :], AF.Identity)
                    gav.read_by(S("act"), nctx)
                npe_end = k.sems["pe"].n
                gk[par].read_by(S("pe"), npe_end)
                gqt[par].read_by(S("pe"), npe_end)
                gvv[vpar].read_by(S("pe"), npe_end)
                gbrow.read_by(S("pe"), npe_end)

            for u in range(len(units)):
                emit_pass1(u)
                if u > 0:
                    emit_pass2(u - 1)
                qb_step((u + 1) * len(qb_tiles) // 48)
                if qb_state[1] >= len(qb_tiles) and not qb_ag_done[0]:
                    qb_ag_done[0] = True
                    k.wait("gpsimd", S("dffnst"), k.sems["dffnst"].n)
                    k.em("gpsimd", lambda E: E.collective_compute(
                        "AllGather", ALU.bypass, replica_groups=RG,
                        ins=[FGU_IN], outs=[FGU_OUT]).then_inc(k.sems["cc"].h, 1))
                    S("cc").inc()
                    k.em("gpsimd", lambda E: E.collective_compute(
                        "AllGather", ALU.bypass, replica_groups=RG,
                        ins=[FD_IN], outs=[FD_OUT]).then_inc(k.sems["cc"].h, 1))
                    S("cc").inc()
            emit_pass2(len(units) - 1)
            assert qb_ag_done[0], "quant-B did not finish within P2"
            n_cc_all = k.sems["cc"].n
            n_act_ao = k.sems["act"].n

        tap("AO", AO[:, :])
        tap("NM", NM[:, :])
        tap("EBF", EBF[:, :])
        tap("NMA", NMA[:, :])
        tap("SIGB", SIGB[:, :])
        tap("LNSB", LNSB[:, :])
        tap("BCOL", BCOL[:, :])
        tap("BROW4", BROW4[:, :])

        with ExitStack() as p3:
            sb3 = lambda n, ss, d: p3.enter_context(nc.sbuf_tensor(n, ss, d))
            WOB = [sb3("wob0", [128, NDMC * 128], BF16),
                   sb3("wob1", [128, NDMC * 128], BF16)]
            XTC = [sb3("xtc0", [128, TOK], F32), sb3("xtc1", [128, TOK], F32)]
            X1B = [sb3("x1b0", [128, TOK], F32), sb3("x1b1", [128, TOK], F32)]
            X1SQ = sb3("x1sq", [128, TOK], BF16)
            MROW = sb3("mrow", [1, TOK], F32)
            M2ROW = sb3("m2row", [1, TOK], F32)
            VROW = sb3("vrow", [1, TOK], F32)
            SROW3 = sb3("srow3", [1, TOK], F32)
            IROW3 = sb3("irow3", [1, TOK], F32)
            psO = [p3.enter_context(nc.psum_tensor("pso%d" % i, [128, 512], F32))
                   for i in range(2)]
            psS1 = p3.enter_context(nc.psum_tensor("pss1", [1, TOK], F32))
            psS2 = p3.enter_context(nc.psum_tensor("pss2", [1, TOK], F32))
            psMB = p3.enter_context(nc.psum_tensor("psmb", [128, TOK], F32))

            gwo = [Res(), Res()]
            gxtc = [Res(), Res()]
            gx1b = [Res(), Res()]
            gx1sq = Res()
            gpso = [Res(), Res()]

            def wob_load(dmc):
                bi = dmc % 2
                k.guard("gpsimd", gwo[bi])
                src = WOT[dmc].rearrange("(a p) f -> p a f", p=128)
                dst = WOB[bi][:, :].rearrange("p (a f) -> p a f", f=128)
                return k.dma("gpsimd", dst, src, S("dwo%d" % bi))

            def xtc_load(dmc):
                bi = dmc % 2
                k.guard("sync", gxtc[bi])
                return k.dma("sync", XTC[bi][:, :],
                             XT[dmc * 128:(dmc + 1) * 128, :], S("dxt%d" % bi))

            k.quiesce()
            nwo = [wob_load(0), wob_load(1)]
            nxt = [xtc_load(0), xtc_load(1)]
            n_x1w = 0
            for dmc in range(NDMC):
                bi = dmc % 2
                k.wait("tensor", S("dwo%d" % bi), nwo[bi])
                nvx = 0
                for half in range(2):
                    k.guard("tensor", gpso[half])
                    for fc in range(NH):
                        k.mm(psO[half][:, :], WOB[bi][:, fc * 128:(fc + 1) * 128],
                             AO[:, fc * TOK + half * 512: fc * TOK + half * 512 + 512],
                             start=(fc == 0), stop=(fc == NH - 1))
                npe = k.sems["pe"].n
                gwo[bi].read_by(S("pe"), npe)
                k.wait("vector", S("pe"), npe)
                k.wait("vector", S("dxt%d" % bi), nxt[bi])
                k.guard("vector", gx1b[bi])
                for half in range(2):
                    k.vop(lambda E, bi=bi, half=half: E.tensor_tensor(
                        X1B[bi][:, half * 512:half * 512 + 512], psO[half][:, :],
                        XTC[bi][:, half * 512:half * 512 + 512], ALU.add))
                nvx = k.vsig()
                for half in range(2):
                    gpso[half].read_by(S("dve"), nvx)
                gxtc[bi].read_by(S("dve"), nvx)
                k.wait("tensor", S("dve"), nvx)
                for half in range(2):
                    k.mm(psS1[0:1, half * 512:half * 512 + 512], ONESF[:, :],
                         X1B[bi][:, half * 512:half * 512 + 512],
                         start=(dmc == 0), stop=(dmc == NDMC - 1), sig=True)
                k.wait("scalar", S("dve"), nvx)
                k.guard("scalar", gx1sq)
                nsq = k.act(X1SQ[:, :], X1B[bi][:, :], AF.Square)
                k.wait("tensor", S("act"), nsq)
                for half in range(2):
                    k.mm(psS2[0:1, half * 512:half * 512 + 512], ONESB[:, :],
                         X1SQ[:, half * 512:half * 512 + 512],
                         start=(dmc == 0), stop=(dmc == NDMC - 1), sig=True)
                npe2 = k.sems["pe"].n
                gx1sq.read_by(S("pe"), npe2)
                k.wait("sync", S("dve"), nvx)
                n_x1w = k.dma("sync", X1SC[dmc * 128:(dmc + 1) * 128, :],
                              X1B[bi][:, :], S("dx1w"))
                gx1b[bi].read_by(S("pe"), npe2)
                gx1b[bi].read_by(S("dx1w"), n_x1w)
                if dmc + 2 < NDMC:
                    nwo[bi] = wob_load(dmc + 2)
                    nxt[bi] = xtc_load(dmc + 2)
            npe3 = k.sems["pe"].n
            k.wait("scalar", S("pe"), npe3)
            k.act(MROW[:, :], psS1[:, :], AF.Identity, scale=1.0 / DM)
            k.act(M2ROW[:, :], psS1[:, :], AF.Square, scale=1.0 / DM)
            na3 = k.act(VROW[:, :], psS2[:, :], AF.Identity, scale=1.0 / DM)
            k.wait("vector", S("act"), na3)
            k.vop(lambda E: E.tensor_tensor(VROW[:, :], VROW[:, :], M2ROW[:, :],
                                            ALU.subtract))
            nv5 = k.vsig()
            k.wait("scalar", S("dve"), nv5)
            na4 = k.act(SROW3[:, :], VROW[:, :], AF.Sqrt, bias=EPS_LN)
            k.wait("vector", S("act"), na4)
            k.vop(lambda E: E.reciprocal(IROW3[:, :], SROW3[:, :]))
            nv6 = k.vsig()
            k.wait("tensor", S("dve"), nv6)
            for half in range(2):
                k.mm(psMB[:, half * 512:half * 512 + 512], ONEROW[:, :],
                     MROW[0:1, half * 512:half * 512 + 512], start=True, stop=True)
            npe4 = k.sems["pe"].n
            k.wait("scalar", S("pe"), npe4)
            k.guard("scalar", gmbb)
            na5 = k.act(MBB[:, :], psMB[:, :], AF.Identity)
            k.wait("tensor", S("act"), na5)
            for half in range(2):
                k.mm(psMB[:, half * 512:half * 512 + 512], ONEROW[:, :],
                     IROW3[0:1, half * 512:half * 512 + 512], start=True, stop=True)
            npe5 = k.sems["pe"].n
            k.wait("scalar", S("pe"), npe5)
            na6 = k.act(IBB[:, :], psMB[:, :], AF.Identity)

        ao_stack.close()

        xb1_stack = ExitStack()
        xb1 = xb1_stack.enter_context(
            nc.sbuf_tensor("xb1", [128, NDMC * TOK], BF16))
        n_xb1 = 0
        n_x1nw = 0
        with ExitStack() as p3b:
            sb3b = lambda n, ss, d: p3b.enter_context(nc.sbuf_tensor(n, ss, d))
            X1R = [sb3b("x1r0", [128, TOK], F32), sb3b("x1r1", [128, TOK], F32)]
            X1NB = [sb3b("x1nb0", [128, TOK], F32), sb3b("x1nb1", [128, TOK], F32)]
            gx1r = [Res(), Res()]
            gx1n = [Res(), Res()]
            nrd = [0, 0]
            k.quiesce()
            k.wait("sync", S("dx1w"), k.sems["dx1w"].n)
            k.wait("sync", S("act"), na6)
            nrd[0] = k.dma("sync", X1R[0][:, :], X1SC[0:128, :], S("dx1r0"))
            nrd[1] = k.dma("sync", X1R[1][:, :], X1SC[128:256, :], S("dx1r1"))
            for dmc in range(NDMC):
                bi = dmc % 2
                k.wait("vector", S("dx1r%d" % bi), nrd[bi])
                k.wait("vector", S("act"), na6)
                k.guard("vector", gx1n[bi])
                k.vop(lambda E, bi=bi: E.tensor_tensor(
                    X1NB[bi][:, :], X1R[bi][:, :], MBB[:, :], ALU.subtract))
                k.vop(lambda E, bi=bi: E.tensor_tensor(
                    X1NB[bi][:, :], X1NB[bi][:, :], IBB[:, :], ALU.mult))
                k.vop(lambda E, bi=bi, dmc=dmc: E.tensor_copy(
                    xb1[:, dmc * TOK:(dmc + 1) * TOK], X1NB[bi][:, :]))
                nvr = k.vsig()
                gx1r[bi].read_by(S("dve"), nvr)
                if dmc == NDMC - 1:
                    gmbb.read_by(S("dve"), nvr)
                k.wait("sync", S("dve"), nvr)
                n_x1nw = k.dma("sync", X1N[dmc * 128:(dmc + 1) * 128, :],
                               X1NB[bi][:, :], S("dx1nw"))
                gx1n[bi].read_by(S("dx1nw"), n_x1nw)
                if dmc + 2 < NDMC:
                    k.guard("sync", gx1r[bi])
                    nrd[bi] = k.dma("sync", X1R[bi][:, :],
                                    X1SC[(dmc + 2) * 128:(dmc + 3) * 128, :],
                                    S("dx1r%d" % bi))
            n_xb1 = k.sems["dve"].n

        with ExitStack() as p4:
            sb4 = lambda n, ss, d: p4.enter_context(nc.sbuf_tensor(n, ss, d))
            GB = [sb4("gb0", [128, NDMC * 128], BF16),
                  sb4("gb1", [128, NDMC * 128], BF16)]
            UB = [sb4("ub0", [128, NDMC * 128], BF16),
                  sb4("ub1", [128, NDMC * 128], BF16)]
            GF = [sb4("gf0", [128, 512], F32), sb4("gf1", [128, 512], F32)]
            INTB = [sb4("intb0", [128, 512], BF16), sb4("intb1", [128, 512], BF16)]
            psG = [p4.enter_context(nc.psum_tensor("psg%d" % i, [128, 512], F32))
                   for i in range(2)]
            psU = [p4.enter_context(nc.psum_tensor("psu%d" % i, [128, 512], F32))
                   for i in range(2)]
            ggb = [Res(), Res()]
            gub = [Res(), Res()]
            ggf = [Res(), Res()]
            gintb = [Res(), Res()]
            gpsg = [Res(), Res()]
            gpsu = [Res(), Res()]
            FGU6 = FGU_OUT.rearrange("(r sec pc ci p) f -> r sec pc ci p f",
                                     r=NCORE, sec=2, pc=4, p=128)

            def gu_load(ci, which, bi):
                gr = (ggb if which == 0 else gub)[bi]
                k.guard("sync", gr)
                buf = (GB if which == 0 else UB)[bi]
                nn_ = 0
                for r in range(NCORE):
                    src = FGU6[r, which, :, ci, :, :]
                    dst = buf[:, r * 512:(r + 1) * 512].rearrange(
                        "p (pc f) -> p pc f", f=128)
                    nn_ = k.dma("sync", dst, src.transpose([1, 0, 2]),
                                S("dwg%d" % bi))
                return nn_

            k.quiesce()
            k.wait("sync", S("cc"), n_cc_all)
            ngb = [gu_load(0, 0, 0), 0]
            nub = [gu_load(0, 1, 0), 0]
            ngb[1] = gu_load(1, 0, 1)
            nub[1] = gu_load(1, 1, 1)
            k.wait("tensor", S("dve"), n_xb1)
            n_intw = 0
            for ci in range(NCI):
                bi = ci % 2
                k.wait("tensor", S("dwg%d" % bi), max(ngb[bi], nub[bi]))
                for half in range(2):
                    par = (ci * 2 + half) % 2
                    k.guard("tensor", gpsg[par])
                    k.guard("tensor", gpsu[par])
                    for dmc in range(NDMC):
                        k.mm(psG[par][:, :], GB[bi][:, dmc * 128:(dmc + 1) * 128],
                             xb1[:, dmc * TOK + half * 512: dmc * TOK + half * 512 + 512],
                             start=(dmc == 0), stop=(dmc == NDMC - 1))
                    for dmc in range(NDMC):
                        k.mm(psU[par][:, :], UB[bi][:, dmc * 128:(dmc + 1) * 128],
                             xb1[:, dmc * TOK + half * 512: dmc * TOK + half * 512 + 512],
                             start=(dmc == 0), stop=(dmc == NDMC - 1))
                    npe = k.sems["pe"].n
                    k.wait("scalar", S("pe"), npe)
                    k.guard("scalar", ggf[par])
                    ngf = k.act(GF[par][:, :], psG[par][:, :], AF.Identity)
                    gpsg[par].read_by(S("act"), ngf)
                    k.wait("vector", S("act"), ngf)
                    k.guard("vector", gintb[par])
                    k.vop(lambda E, par=par: E.tensor_tensor(
                        INTB[par][:, :], psU[par][:, :], GF[par][:, :], ALU.mult))
                    nvi = k.vsig()
                    gpsu[par].read_by(S("dve"), nvi)
                    ggf[par].read_by(S("dve"), nvi)
                    k.wait("sync", S("dve"), nvi)
                    n_intw = k.dma("sync",
                                   INTER[ci * 128:(ci + 1) * 128,
                                         half * 512:half * 512 + 512],
                                   INTB[par][:, :], S("dintw"))
                    gintb[par].read_by(S("dintw"), n_intw)
                npe_ci = k.sems["pe"].n
                ggb[bi].read_by(S("pe"), npe_ci)
                gub[bi].read_by(S("pe"), npe_ci)
                if ci + 2 < NCI:
                    ngb[bi] = gu_load(ci + 2, 0, bi)
                    nub[bi] = gu_load(ci + 2, 1, bi)
        xb1_stack.close()

        with ExitStack() as p5:
            sb5 = lambda n, ss, d: p5.enter_context(nc.sbuf_tensor(n, ss, d))
            INTH = sb5("inth", [128, NCI * 512], BF16)
            WDS = [sb5("wds0", [128, NCI * 128], BF16),
                   sb5("wds1", [128, NCI * 128], BF16)]
            XNC = [sb5("xnc0", [128, 512], F32), sb5("xnc1", [128, 512], F32)]
            DF = [sb5("df0", [128, 512], F32), sb5("df1", [128, 512], F32)]
            X2B = [sb5("x2b0", [128, 512], F32), sb5("x2b1", [128, 512], F32)]
            X2SQ = sb5("x2sq", [128, 512], BF16)
            X2R = [sb5("x2r0", [128, 512], F32), sb5("x2r1", [128, 512], F32)]
            OUTB = [sb5("outb0", [128, 512], F32), sb5("outb1", [128, 512], F32)]
            MR5 = sb5("mr5", [1, 512], F32)
            M25 = sb5("m25", [1, 512], F32)
            VR5 = sb5("vr5", [1, 512], F32)
            SR5 = sb5("sr5", [1, 512], F32)
            IR5 = sb5("ir5", [1, 512], F32)
            psD = [p5.enter_context(nc.psum_tensor("psd%d" % i, [128, 512], F32))
                   for i in range(2)]
            psS1b = p5.enter_context(nc.psum_tensor("pss1b", [1, 512], F32))
            psS2b = p5.enter_context(nc.psum_tensor("pss2b", [1, 512], F32))
            psMB5 = p5.enter_context(nc.psum_tensor("psmb5", [128, 512], F32))

            gwds = [Res(), Res()]
            gxnc = [Res(), Res()]
            gdf = [Res(), Res()]
            gx2b = [Res(), Res()]
            gx2sq = Res()
            gx2r = [Res(), Res()]
            goutb = [Res(), Res()]
            gpsd = [Res(), Res()]
            ginth = Res()

            def wds_load(dmc):
                bi = dmc % 2
                r, dd = dmc >> 2, dmc & 3
                k.guard("sync", gwds[bi])
                src = FD_OUT[r * 4 * IFD + dd * IFD: r * 4 * IFD + (dd + 1) * IFD,
                             :].rearrange("(ci p) f -> p ci f", p=128)
                dst = WDS[bi][:, :].rearrange("p (ci f) -> p ci f", f=128)
                return k.dma("sync", dst, src, S("dwd%d" % bi))

            n_out = 0
            k.quiesce()
            for tp in range(2):
                k.guard("sync", ginth)
                k.wait("sync", S("dintw"), k.sems["dintw"].n)
                nih = k.dma("sync", INTH[:, :].rearrange("p (ci t) -> p ci t", t=512),
                            INTER.rearrange("(ci p) t -> p ci t", p=128)[
                                :, :, tp * 512:tp * 512 + 512], S("dintr0"))
                nwd = [wds_load(0), wds_load(1)]
                nxn = [0, 0]
                k.wait("sync", S("dx1nw"), k.sems["dx1nw"].n)
                nxn[0] = k.dma("sync", XNC[0][:, :],
                               X1N[0:128, tp * 512:tp * 512 + 512], S("dxn0"))
                nxn[1] = k.dma("sync", XNC[1][:, :],
                               X1N[128:256, tp * 512:tp * 512 + 512], S("dxn1"))
                k.wait("tensor", S("dintr0"), nih)
                n_x2w = 0
                for dmc in range(NDMC):
                    bi = dmc % 2
                    k.wait("tensor", S("dwd%d" % bi), nwd[bi])
                    k.guard("tensor", gpsd[bi])
                    for ci in range(NCI):
                        k.mm(psD[bi][:, :], WDS[bi][:, ci * 128:(ci + 1) * 128],
                             INTH[:, ci * 512:(ci + 1) * 512],
                             start=(ci == 0), stop=(ci == NCI - 1))
                    npe = k.sems["pe"].n
                    gwds[bi].read_by(S("pe"), npe)
                    k.wait("scalar", S("pe"), npe)
                    k.guard("scalar", gdf[bi])
                    ndf = k.act(DF[bi][:, :], psD[bi][:, :], AF.Identity,
                                scale=S3C[:, :])
                    gpsd[bi].read_by(S("act"), ndf)
                    k.wait("vector", S("act"), ndf)
                    k.wait("vector", S("dxn%d" % bi), nxn[bi])
                    k.guard("vector", gx2b[bi])
                    k.vop(lambda E, bi=bi: E.tensor_tensor(
                        X2B[bi][:, :], DF[bi][:, :], XNC[bi][:, :], ALU.add))
                    nvx2 = k.vsig()
                    gdf[bi].read_by(S("dve"), nvx2)
                    gxnc[bi].read_by(S("dve"), nvx2)
                    k.wait("tensor", S("dve"), nvx2)
                    k.mm(psS1b[:, :], ONESF[:, :], X2B[bi][:, :],
                         start=(dmc == 0), stop=(dmc == NDMC - 1), sig=True)
                    k.wait("scalar", S("dve"), nvx2)
                    k.guard("scalar", gx2sq)
                    nsq2 = k.act(X2SQ[:, :], X2B[bi][:, :], AF.Square)
                    k.wait("tensor", S("act"), nsq2)
                    k.mm(psS2b[:, :], ONESB[:, :], X2SQ[:, :],
                         start=(dmc == 0), stop=(dmc == NDMC - 1), sig=True)
                    npe2 = k.sems["pe"].n
                    gx2sq.read_by(S("pe"), npe2)
                    k.wait("sync", S("dve"), nvx2)
                    n_x2w = k.dma("sync",
                                  X2SC[dmc * 128:(dmc + 1) * 128,
                                       tp * 512:tp * 512 + 512],
                                  X2B[bi][:, :], S("dx2w"))
                    gx2b[bi].read_by(S("pe"), npe2)
                    gx2b[bi].read_by(S("dx2w"), n_x2w)
                    if dmc + 2 < NDMC:
                        nwd[bi] = wds_load(dmc + 2)
                        k.guard("sync", gxnc[bi])
                        nxn[bi] = k.dma("sync", XNC[bi][:, :],
                                        X1N[(dmc + 2) * 128:(dmc + 3) * 128,
                                            tp * 512:tp * 512 + 512],
                                        S("dxn%d" % bi))
                ginth.read_by(S("pe"), k.sems["pe"].n)
                npe3 = k.sems["pe"].n
                k.wait("scalar", S("pe"), npe3)
                k.act(MR5[:, :], psS1b[:, :], AF.Identity, scale=1.0 / DM)
                k.act(M25[:, :], psS1b[:, :], AF.Square, scale=1.0 / DM)
                na3 = k.act(VR5[:, :], psS2b[:, :], AF.Identity, scale=1.0 / DM)
                k.wait("vector", S("act"), na3)
                k.vop(lambda E: E.tensor_tensor(VR5[:, :], VR5[:, :], M25[:, :],
                                                ALU.subtract))
                nv5 = k.vsig()
                k.wait("scalar", S("dve"), nv5)
                na4 = k.act(SR5[:, :], VR5[:, :], AF.Sqrt, bias=EPS_LN)
                k.wait("vector", S("act"), na4)
                k.vop(lambda E: E.reciprocal(IR5[:, :], SR5[:, :]))
                nv6 = k.vsig()
                k.wait("tensor", S("dve"), nv6)
                k.mm(psMB5[:, :], ONEROW[:, :], MR5[:, :], start=True, stop=True)
                npe4 = k.sems["pe"].n
                k.wait("scalar", S("pe"), npe4)
                k.guard("scalar", gmbb)
                na5 = k.act(MBB[:, 0:512], psMB5[:, :], AF.Identity)
                k.wait("tensor", S("act"), na5)
                k.mm(psMB5[:, :], ONEROW[:, :], IR5[:, :], start=True, stop=True)
                npe5 = k.sems["pe"].n
                k.wait("scalar", S("pe"), npe5)
                na6b = k.act(IBB[:, 0:512], psMB5[:, :], AF.Identity)
                nr2 = [0, 0]
                k.wait("sync", S("dx2w"), k.sems["dx2w"].n)
                k.guard("sync", gx2r[0])
                nr2[0] = k.dma("sync", X2R[0][:, :],
                               X2SC[0:128, tp * 512:tp * 512 + 512], S("dx2r0"))
                k.guard("sync", gx2r[1])
                nr2[1] = k.dma("sync", X2R[1][:, :],
                               X2SC[128:256, tp * 512:tp * 512 + 512], S("dx2r1"))
                for dmc in range(NDMC):
                    bi = dmc % 2
                    k.wait("vector", S("dx2r%d" % bi), nr2[bi])
                    k.wait("vector", S("act"), na6b)
                    k.guard("vector", goutb[bi])
                    k.vop(lambda E, bi=bi: E.tensor_tensor(
                        OUTB[bi][:, :], X2R[bi][:, :], MBB[:, 0:512], ALU.subtract))
                    k.vop(lambda E, bi=bi: E.tensor_tensor(
                        OUTB[bi][:, :], OUTB[bi][:, :], IBB[:, 0:512], ALU.mult))
                    nvo = k.vsig()
                    gx2r[bi].read_by(S("dve"), nvo)
                    if dmc == NDMC - 1:
                        gmbb.read_by(S("dve"), nvo)
                    k.wait("sync", S("dve"), nvo)
                    n_out = k.dma("sync",
                                  OUT[dmc * 128:(dmc + 1) * 128,
                                      tp * 512:tp * 512 + 512],
                                  OUTB[bi][:, :], S("dout"))
                    goutb[bi].read_by(S("dout"), n_out)
                    if dmc + 2 < NDMC:
                        k.guard("sync", gx2r[bi])
                        nr2[bi] = k.dma("sync", X2R[bi][:, :],
                                        X2SC[(dmc + 2) * 128:(dmc + 3) * 128,
                                             tp * 512:tp * 512 + 512],
                                        S("dx2r%d" % bi))
                nv_fin = k.sems["dve"].n

        tap("X1N", X1N)
        tap("INVC", INVC[:, 0:3])
        tap("FDIN", FD_IN)
        tap("FGUIN", FGU_IN)
        tap("X1SC", X1SC)
        tap("INTER", INTER)
        tap("X2SC", X2SC)
        with ExitStack() as pz:
            k.wait("sync", S("cc"), k.sems["cc"].n)
            k.wait("gpsimd", S("cc"), k.sems["cc"].n)
            k.wait("gpsimd", S("dve"), k.sems["dve"].n)
            k.wait("gpsimd", S("act"), k.sems["act"].n)
            k.wait("sync", S("act"), k.sems["act"].n)
            k.wait("sync", S("dve"), k.sems["dve"].n)
            k.wait("gpsimd", S("dqst"), k.sems["dqst"].n)
            for (t, srcap) in tap_list:
                q = "sync" if srcap.dtype == F32 else "gpsimd"
                k.dma(q, t, srcap, S("dout"))
            k.wait("sync", S("dout"), k.sems["dout"].n)

        with ExitStack() as sc:
            for s in k.sems.values():
                s.h = sc.enter_context(nc.semaphore(s.name))
            block = sc.enter_context(nc.Block())
            k.emit(block)

    return nc


def _prep_shared(Wq, Wk, Wv, Wo, w_gate, w_up, w_down):
    d = {}
    d["WQT"] = np.ascontiguousarray(Wq.T.reshape(DM, NH, HD).transpose(1, 0, 2))
    d["WKT"] = np.ascontiguousarray(Wk.T.reshape(DM, NKV, HD).transpose(1, 0, 2))
    d["WVT"] = np.ascontiguousarray(Wv.T.reshape(DM, 2, 512).transpose(1, 0, 2))
    d["WOT"] = np.ascontiguousarray(
        np.ascontiguousarray(Wo.T).reshape(DM, NDMC, 128).transpose(1, 0, 2))
    d["WGT_full"] = np.ascontiguousarray(w_gate.T)
    d["WUT_full"] = np.ascontiguousarray(w_up.T)
    d["WDT_full"] = np.ascontiguousarray(w_down.T)
    return d


def _masks_for_core(c):
    m1 = np.zeros((128, 1024), dtype=np.float32)
    for w in range(8):
        if w > c:
            m1[127, w * 128:(w + 1) * 128] = NEGBIG
        elif w == c:
            for u in range(1, 128):
                m1[u - 1, w * 128 + u] = NEGBIG
    m2 = np.zeros((128, 1024), dtype=np.float32)
    for w in range(8):
        if w > c:
            m2[127, w * 128:(w + 1) * 128] = NEGBIG
        elif w == c:
            for s_ in range(1, 128):
                m2[s_ - 1, w * 128 + s_] = NEGBIG
    return m1, m2


def _inputs_for_core(c, x, shared, qnw, knw):
    rows = []
    for t in range(NSLOT):
        b, J = slot_bj(t, c)
        rows.append(x[b, J * 128:(J + 1) * 128, :])
    xt = np.ascontiguousarray(np.concatenate(rows, axis=0).T)
    m1, m2 = _masks_for_core(c)
    return {
        "XT": xt,
        "WQT": shared["WQT"], "WKT": shared["WKT"], "WVT": shared["WVT"],
        "WOT": shared["WOT"],
        "WG": np.ascontiguousarray(shared["WGT_full"][512 * c:512 * (c + 1), :]),
        "WU": np.ascontiguousarray(shared["WUT_full"][512 * c:512 * (c + 1), :]),
        "WD": np.ascontiguousarray(shared["WDT_full"][:, 512 * c:512 * (c + 1)]),
        "QNW": np.ascontiguousarray(qnw.reshape(128, 1)),
        "KNW": np.ascontiguousarray(knw.reshape(128, 1)),
        "MSK1": m1, "MSK2": m2,
    }


def make_in_maps(L0_attn_x, L0_attn_Wq, L0_attn_Wk, L0_attn_Wv, L0_attn_Wo,
                 L0_attn_q_norm_w, L0_attn_k_norm_w,
                 L0_ffn_w_gate, L0_ffn_w_up, L0_ffn_w_down):
    x = np.asarray(L0_attn_x, dtype=np.float32)
    shared = _prep_shared(np.asarray(L0_attn_Wq, np.float32),
                          np.asarray(L0_attn_Wk, np.float32),
                          np.asarray(L0_attn_Wv, np.float32),
                          np.asarray(L0_attn_Wo, np.float32),
                          np.asarray(L0_ffn_w_gate, np.float32),
                          np.asarray(L0_ffn_w_up, np.float32),
                          np.asarray(L0_ffn_w_down, np.float32))
    qnw = np.asarray(L0_attn_q_norm_w, np.float32)
    knw = np.asarray(L0_attn_k_norm_w, np.float32)
    return [_inputs_for_core(c, x, shared, qnw, knw) for c in range(NCORE)]


def run(in_maps, taps=()):
    nc = build(taps=taps)
    return run_bass_kernel_spmd(nc, in_maps, core_ids=list(range(NCORE)))


def kernel(**inputs):
    in_maps = make_in_maps(**inputs)
    res = run(in_maps)
    out = np.empty((BATCH, SEQ, DM), dtype=np.float32)
    for c in range(NCORE):
        oc = res.results[c]["OUT"]
        for t in range(NSLOT):
            b, J = slot_bj(t, c)
            out[b, J * 128:(J + 1) * 128, :] = oc[:, t * 128:(t + 1) * 128].T
    return out
